# version 08 (final)
# baseline (speedup 1.0000x reference)
import numpy as np
import ml_dtypes
import concourse.bass as bass
import concourse.mybir as mybir
from concourse.bass_utils import run_bass_kernel_spmd

F32 = mybir.dt.float32
BF16 = mybir.dt.bfloat16
AF = mybir.ActivationFunctionType
ALU = mybir.AluOpType

B, S, H = 4, 2048, 2048
NK, NV, DK, DV = 8, 16, 64, 64
KDIM, VDIM = NK * DK, NV * DV
NFF = 8192
TOWN, THALO = 1024, 1024
TT = TOWN + THALO
CH = 64
NCH_OWN, NCH_ALL = TOWN // CH, TT // CH
NHP = 8
EPS_L2, EPS_LN = 1e-6, 1e-5
SLAB = 512

PC_K = list(range(0, 4))
PC_V = list(range(4, 12))
PC_BG = [12]
PC_Q = list(range(13, 17))
PC_Z = list(range(17, 25))
NPC = 25
PROJ_COLS = NPC * 128
HALO_PC = PC_K + PC_V + PC_BG
NLEV = 6

DBG = {"mach_n": 999, "mach_parts": 15}

A_BETA, A_B, A_BLOGB, A_EBLOGB, A_EBAR = 0, 1, 2, 3, 4
A_G = 4
XARR = [A_B, A_BLOGB, A_EBLOGB, A_BETA, A_EBAR]
XI_NEGB, XI_BLOGB, XI_EBLOGB, XI_BETA, XI_EBAR = 0, 1, 2, 3, 4


def _host_consts():
    c = {}
    ident = np.eye(128, dtype=np.float32)
    c["identb"] = ident.astype(ml_dtypes.bfloat16)
    c["identf"] = ident.copy()
    sel16 = np.zeros((16, 128), dtype=np.float32)
    for i in range(16):
        sel16[i, :] = ((np.arange(128) // 64) == (i % 2)).astype(np.float32)
    c["sel16"] = sel16.astype(ml_dtypes.bfloat16)
    c["sel16f"] = sel16.copy()
    c["onesf"] = np.ones((1, 128), dtype=np.float32)
    selhp = np.zeros((8, 16, 128), dtype=np.float32)
    for hp in range(8):
        for p in range(128):
            selhp[hp, 2 * hp + p // 64, p] = 1.0
    c["selhp"] = selhp
    bo16 = np.zeros((8, 128, 16), dtype=np.float32)
    for j in range(8):
        for half in range(2):
            bo16[j, half * 64:(half + 1) * 64, 2 * j + half] = 1.0
    c["bo16"] = bo16.astype(ml_dtypes.bfloat16)
    p64 = (np.arange(128) % 64)[:, None]
    t64 = np.arange(64)[None, :]
    c["mus"] = (p64 < t64).astype(np.uint8)
    c["mls"] = (t64 < p64).astype(np.uint8)
    c["mui"] = (p64 <= t64).astype(np.uint8)
    c["is2"] = (p64 == t64).astype(np.float32)
    pd = np.zeros((2, 128, 128), dtype=np.float32)
    for par in range(2):
        for p in range(128):
            pd[par, par * 64 + (p % 64), p] = 1.0
    c["permd2"] = pd.astype(ml_dtypes.bfloat16)
    ph = np.zeros((2, 128, 64), dtype=np.float32)
    for par in range(2):
        for m in range(64):
            ph[par, par * 64 + m, m] = 1.0
    c["permh"] = ph.astype(ml_dtypes.bfloat16)
    c["ones"] = np.ones((128, 128), dtype=np.float32).astype(ml_dtypes.bfloat16)
    return c


def host_prep(inputs):
    x = np.asarray(inputs["L0_attn_hidden_states"], dtype=np.float32)
    Wqkv = np.asarray(inputs["L0_attn_W_qkv"], dtype=np.float32)
    convw = np.asarray(inputs["L0_attn_conv_w"], dtype=np.float32)[:, 0, :]
    Wz = np.asarray(inputs["L0_attn_W_z"], dtype=np.float32)
    Wb = np.asarray(inputs["L0_attn_W_b"], dtype=np.float32)
    Wa = np.asarray(inputs["L0_attn_W_a"], dtype=np.float32)
    Wout = np.asarray(inputs["L0_attn_W_out"], dtype=np.float32)

    q_cols = np.ascontiguousarray(Wqkv[0:KDIM].T)
    k_cols = np.ascontiguousarray(Wqkv[KDIM:2 * KDIM].T)
    v_cols = np.ascontiguousarray(Wqkv[2 * KDIM:].T)
    bg = np.zeros((H, 128), dtype=np.float32)
    bg[:, 0:16] = Wb.T
    bg[:, 16:32] = Wa.T
    wproj = np.concatenate([k_cols, v_cols, bg, q_cols, np.ascontiguousarray(Wz.T)], axis=1)

    convq = convw[0:KDIM]
    convk = convw[KDIM:2 * KDIM]
    convv = convw[2 * KDIM:]
    conv_chunks = [convk[128 * i:128 * (i + 1)] for i in range(4)] + \
                  [convv[128 * i:128 * (i + 1)] for i in range(8)] + \
                  [convq[128 * i:128 * (i + 1)] for i in range(4)]
    convw_sb = np.concatenate(conv_chunks, axis=1)

    consts = _host_consts()
    shared = {
        "wproj": np.ascontiguousarray(
            wproj.reshape(16, 128, PROJ_COLS // 128, 128)
            .transpose(2, 1, 0, 3).reshape(PROJ_COLS, H)),
        "convw": convw_sb,
        "woutT": np.ascontiguousarray(Wout.T),
        "w0T": np.ascontiguousarray(np.asarray(inputs["L0_ffn_W0"], dtype=np.float32).T),
        "w1T": np.ascontiguousarray(np.asarray(inputs["L0_ffn_W1"], dtype=np.float32).T),
        "bias0": np.ascontiguousarray(
            np.asarray(inputs["L0_ffn_bias0"], dtype=np.float32).reshape(64, 128).T),
        "bias1": np.ascontiguousarray(
            np.asarray(inputs["L0_ffn_bias1"], dtype=np.float32).reshape(16, 128).T),
        "dtb": np.asarray(inputs["L0_attn_dt_bias"], dtype=np.float32)[:, None],
        "alog": np.asarray(inputs["L0_attn_A_log"], dtype=np.float32)[:, None],
        "nw2": np.tile(np.asarray(inputs["L0_attn_norm_weight"], dtype=np.float32), 2)[:, None],
        "identb": consts["identb"], "identf": consts["identf"], "sel16": consts["sel16"],
        "sel16f": consts["sel16f"], "onesf": consts["onesf"],
        "selhp": np.ascontiguousarray(consts["selhp"].reshape(8 * 16, 128)),
        "bo16": np.ascontiguousarray(consts["bo16"].reshape(8 * 128, 16)),
        "mus": consts["mus"], "mls": consts["mls"], "mui": consts["mui"],
        "is2": consts["is2"],
        "permd2": np.ascontiguousarray(consts["permd2"].reshape(2 * 128, 128)),
        "permh": np.ascontiguousarray(consts["permh"].reshape(2 * 128, 64)),
        "ones": consts["ones"],
    }
    zeros_halo = np.zeros((H, THALO), dtype=np.float32)
    in_maps = []
    for c in range(8):
        b, half = c // 2, c % 2
        xT = np.ascontiguousarray(x[b].T)
        m = dict(shared)
        m["xin"] = np.ascontiguousarray(xT[:, half * TOWN:(half + 1) * TOWN])
        m["xhalo"] = np.ascontiguousarray(xT[:, 0:THALO]) if half == 1 else zeros_halo
        in_maps.append(m)
    return in_maps


class Buf:
    __slots__ = ("writer", "readers", "name")

    def __init__(self, name=""):
        self.writer = None
        self.readers = {}
        self.name = name


class KB:
    ENGS = ("T", "V", "S", "G", "Q")

    NDMA_G, NDMA_Q = 14, 6

    def __init__(self, nc, es):
        self.nc = nc
        self.eng = {"T": nc.tensor, "V": nc.vector, "S": nc.scalar,
                    "G": nc.gpsimd, "Q": nc.sync}
        self.sems = {n: es.enter_context(nc.semaphore("sem_" + n))
                     for n in ("T", "V", "S", "G")}
        for i in range(self.NDMA_G):
            self.sems[f"DG{i}"] = es.enter_context(nc.semaphore(f"sem_DG{i}"))
        for i in range(self.NDMA_Q):
            self.sems[f"DQ{i}"] = es.enter_context(nc.semaphore(f"sem_DQ{i}"))
        self.cnt = {k: 0 for k in self.sems}
        self.dma_i = {"G": 0, "Q": 0}
        self.vc = {e: {} for e in self.ENGS}
        self.on = True
        self.drained = {e: 0 for e in self.ENGS}

    def _wait(self, eng, sem, val):
        if val <= 0 or sem == eng:
            return
        if self.vc[eng].get(sem, 0) >= val:
            return
        self.eng[eng].wait_ge(self.sems[sem], val)
        self.vc[eng][sem] = val

    def emit(self, eng, fn, reads=(), writes=(), dma=False):
        if not self.on:
            return None
        if eng in ("V", "S", "G"):
            for b in reads:
                if (b.writer and b.writer[0] == eng
                        and self.cnt[eng] - b.writer[1] <= 1
                        and self.drained[eng] < b.writer[1]):
                    self.eng[eng].drain()
                    self.drained[eng] = self.cnt[eng]
                    break
        need = {}
        for b in reads:
            if b.writer and b.writer[1] > need.get(b.writer[0], 0):
                need[b.writer[0]] = b.writer[1]
        for b in writes:
            if b.writer and b.writer[1] > need.get(b.writer[0], 0):
                need[b.writer[0]] = b.writer[1]
            for sem, val in b.readers.items():
                if val > need.get(sem, 0):
                    need[sem] = val
        if dma:
            q = "Q" if eng == "Q" else "G"
            n = self.NDMA_Q if q == "Q" else self.NDMA_G
            dsem = f"D{q}{self.dma_i[q] % n}"
            self.dma_i[q] += 1
            if self.cnt[dsem] > need.get(dsem, 0):
                need[dsem] = self.cnt[dsem]
        for sem in sorted(need):
            self._wait(eng, sem, need[sem])
        instr = fn(self.eng[eng])
        if dma:
            self.cnt[dsem] += 16
            instr.then_inc(self.sems[dsem], 16)
            tok = (dsem, self.cnt[dsem])
        else:
            self.cnt[eng] += 1
            instr.then_inc(self.sems[eng], 1)
            tok = (eng, self.cnt[eng])
        for b in reads:
            if b.readers.get(tok[0], 0) < tok[1]:
                b.readers[tok[0]] = tok[1]
        for b in writes:
            b.writer = tok
            b.readers = {}
        return tok

    def barrier(self):
        snap = dict(self.cnt)
        for e in self.ENGS:
            for sem, val in snap.items():
                if sem != e and val > 0:
                    self._wait(e, sem, val)

    def finish(self):
        snap = dict(self.cnt)
        for sem, val in snap.items():
            if sem != "Q" and val > 0:
                self._wait("Q", sem, val)


class PPool:

    def __init__(self, banks):
        self.banks = banks
        self.cur = 0
        self.off = [0] * len(banks)

    def alloc(self, w):
        for _ in range(2 * len(self.banks) + 1):
            t, c0, ncols, buf = self.banks[self.cur]
            if self.off[self.cur] + w <= ncols:
                off = self.off[self.cur]
                self.off[self.cur] += w
                ap = t[:, c0 + off:c0 + off + w]
                self.cur = (self.cur + 1) % len(self.banks)
                return ap, buf
            self.off[self.cur] = 0
        raise RuntimeError("psum pool alloc")


def build_nc(debug=False, phase=8):
    from contextlib import ExitStack
    nc = bass.Bass("TRN2", target_bir_lowering=False)
    es = ExitStack()
    dr = {}

    def din(name, shape, dt=F32):
        dr[name] = nc.dram_tensor(name, list(shape), dt, kind="ExternalInput").ap()

    din("xin", (H, TOWN)); din("xhalo", (H, THALO))
    din("wproj", (PROJ_COLS, H)); din("convw", (128, 64))
    din("woutT", (VDIM, H)); din("w0T", (H, NFF)); din("w1T", (NFF, H))
    din("bias0", (128, 64)); din("bias1", (128, 16))
    din("dtb", (16, 1)); din("alog", (16, 1)); din("nw2", (128, 1))
    din("identb", (128, 128), BF16); din("identf", (128, 128))
    din("sel16", (16, 128), BF16); din("bo16", (8 * 128, 16), BF16)
    din("sel16f", (16, 128)); din("onesf", (1, 128))
    din("selhp", (8 * 16, 128))
    din("mus", (128, 64), mybir.dt.uint8); din("mls", (128, 64), mybir.dt.uint8)
    din("mui", (128, 64), mybir.dt.uint8); din("is2", (128, 64))
    din("permd2", (2 * 128, 128), BF16); din("permh", (2 * 128, 64), BF16)
    din("ones", (128, 128), BF16)
    yout_d = nc.dram_tensor("yout", [H, TOWN], F32, kind="ExternalOutput").ap()

    dbg_specs = [("d_raws", 128, 2 * 520), ("d_tails", 128, 64),
                 ("d_khat", 128, 4 * TT), ("d_qx", 128, 4 * TOWN),
                 ("d_bg", 16, 5 * TT), ("d_ohat", 128, 8 * TOWN),
                 ("d_zs", 128, 8 * TOWN), ("d_attn", 128, 16 * TOWN),
                 ("d_y1", 128, 16 * TOWN), ("d_ktm", 128, 16 * 512),
                 ("d_xcol", 128, NCH_ALL * 40)]
    dbg = {}
    if debug:
        for name, p, f in dbg_specs:
            dbg[name] = nc.dram_tensor(name, [p, f], F32, kind="ExternalOutput").ap()

    with es:
        kb = KB(nc, es)

        def sb(name, p, f, dt=F32):
            return es.enter_context(nc.sbuf_tensor(name, [p, f], dt))

        cn = {}
        for name, p, f, dt in [
                ("convw", 128, 64, F32), ("bias0", 128, 64, F32), ("bias1", 128, 16, F32),
                ("dtb", 16, 1, F32), ("alog", 16, 1, F32), ("nw2", 128, 1, F32),
                ("identb", 128, 128, BF16), ("identf", 128, 128, F32),
                ("sel16", 16, 128, BF16), ("sel16f", 16, 128, F32),
                ("onesf", 1, 128, F32),
                ("mus", 128, 64, mybir.dt.uint8), ("mls", 128, 64, mybir.dt.uint8),
                ("mui", 128, 64, mybir.dt.uint8), ("is2", 128, 64, F32),
                ("ones", 128, 128, BF16)]:
            cn[name] = sb("c_" + name, p, f, dt)
        bo16 = sb("c_bo16", 128, 8 * 16, BF16)
        selhp = sb("c_selhp", 16, 8 * 128, F32)
        permd2 = sb("c_permd2", 128, 2 * 128, BF16)
        permh = sb("c_permh", 128, 2 * 64, BF16)
        cb = {k: Buf("c_" + k) for k in list(cn) + ["bo16", "permd2", "permh", "selhp"]}
        for name in cn:
            kb.emit("Q", lambda e, n=name: e.dma_start(cn[n][:], dr[n]),
                    writes=[cb[name]], dma=True)
        kb.emit("Q", lambda e: e.dma_start(
            bo16[:].rearrange("p (j m) -> p j m", m=16),
            dr["bo16"].rearrange("(j p) m -> p j m", p=128)),
            writes=[cb["bo16"]], dma=True)
        kb.emit("Q", lambda e: e.dma_start(
            selhp[:].rearrange("k (j m) -> k j m", m=128),
            dr["selhp"].rearrange("(j k) m -> k j m", k=16)),
            writes=[cb["selhp"]], dma=True)
        kb.emit("Q", lambda e: e.dma_start(
            permd2[:].rearrange("p (j m) -> p j m", m=128),
            dr["permd2"].rearrange("(j p) m -> p j m", p=128)),
            writes=[cb["permd2"]], dma=True)
        kb.emit("Q", lambda e: e.dma_start(
            permh[:].rearrange("p (j m) -> p j m", m=64),
            dr["permh"].rearrange("(j p) m -> p j m", p=128)),
            writes=[cb["permh"]], dma=True)

        pp = es.enter_context(nc.psum_tensor("pp", [128, 1024], F32))
        pm = es.enter_context(nc.psum_tensor("pm", [128, 512], F32))
        pmb = es.enter_context(nc.psum_tensor("pmb", [128, 1024], BF16))
        psA = es.enter_context(nc.psum_tensor("psA", [128, 512], F32))
        psB = es.enter_context(nc.psum_tensor("psB", [128, 512], F32))
        poA = es.enter_context(nc.psum_tensor("poA", [128, 512], F32))
        poB = es.enter_context(nc.psum_tensor("poB", [128, 512], F32))
        b_pp0, b_pp1 = Buf("pp0"), Buf("pp1")
        b_pm, b_pmb = Buf("pm"), Buf("pmb")
        b_psA, b_psB, b_poA, b_poB = Buf("psA"), Buf("psB"), Buf("poA"), Buf("poB")
        pmbf = pmb[:].bitcast(F32)
        pool = PPool([(pp, 0, 512, b_pp0), (pp, 512, 512, b_pp1),
                      (pmbf, 0, 512, b_pmb)])
        poolS = PPool([(pm, 0, 512, b_pm)])
        poolb = PPool([(pmb, 0, 1024, b_pmb)])

        ABF = sb("ABF", 128, 55296, BF16)
        AFA = sb("AFA", 128, 19840, F32)

        def abf(off, n, p0=0, p1=128):
            return ABF[p0:p1, off:off + n]

        def af(off, n, p0=0, p1=128):
            return AFA[p0:p1, off:off + n]

        def inherit(newbufs, oldbufs):
            if isinstance(newbufs, Buf):
                newbufs = [newbufs]
            for nb in newbufs:
                for ob in (oldbufs if not isinstance(oldbufs, Buf) else [oldbufs]):
                    if ob.writer and ob.writer[1] > nb.readers.get(ob.writer[0], 0):
                        nb.readers[ob.writer[0]] = ob.writer[1]
                    for sem, val in ob.readers.items():
                        if val > nb.readers.get(sem, 0):
                            nb.readers[sem] = val

        O_XS = 0
        O_Q = 16384
        O_KHAT = 20480
        O_VHAT = 28672
        O_OHAT = 28672
        O_MACH = 36864
        O_ZS = 36864
        O_WBUF, O_RAWS, O_DTILE = 45056, 49152, 50192
        O_SQSC = 51216
        O_KTM = 45056
        O_VTM = 0
        O_OMOV = 0
        O_WOBUF = 20480
        O_Y1B, O_DFF, O_W0B, O_W1B = 0, 16384, 49152, 53248
        O_HM = 0
        O_TMP16, O_BGRAW, O_SSK = 10240, 12288, 14336
        O_SSQ = 8192
        O_XINT = 10240
        O_HMP = 10240
        O_WEX, O_SST = 0, 2048
        O_XIN = 0
        O_XCOL, O_SSO = 16384, 16384
        O_EBCHP = 17664
        O_LNM, O_LNR, O_LNT = 16384, 17664, 18688
        O_MISC = 19712

        xslab = [abf(O_XS + i * 8192, 8192) for i in range(2)]
        b_xslab = [Buf(f"xslab{i}") for i in range(2)]

        khat = abf(O_KHAT, 4 * TT)
        vhat = abf(O_VHAT, 8 * TT)
        qhat = abf(O_Q, 4 * TOWN)
        zsilu = abf(O_ZS, 8 * TOWN)
        bgraw = af(O_BGRAW, TT, 0, 32)
        b_khat = [Buf(f"khat{j}") for j in range(4)]
        b_vhat = [Buf(f"vhat{j}") for j in range(8)]
        b_qhat = [Buf(f"q{j}") for j in range(4)]
        b_zsilu = [Buf(f"zs{j}") for j in range(8)]
        b_bgraw = Buf("bgraw")

        wbuf = [abf(O_WBUF + i * 2048, 2048) for i in range(2)]
        b_wbuf = [Buf(f"wbuf{i}") for i in range(2)]
        raws = [abf(O_RAWS + i * 520, SLAB + 4) for i in range(2)]
        b_raws = [Buf(f"raws{i}") for i in range(2)]
        tails = sb("tails", 128, 16 * 4, F32)
        b_tails = Buf("tails")
        tail3 = sb("tail3", 128, 16 * 4, BF16)
        b_tail3 = Buf("tail3")
        dtile = [abf(O_DTILE + i * 512, 4 * 128) for i in range(2)]
        b_dtile = [Buf(f"dt{i}") for i in range(2)]

        def wpanel_load(oc, slot):
            if oc in PC_Z:
                src = wpcast[oc - PC_Z[0]]
                kb.emit("Q", lambda e, slot=slot, src=src: e.dma_start(
                    wbuf[slot][:], src),
                    reads=[b_wpc], writes=[b_wbuf[slot]], dma=True)
                return
            src = dr["wproj"][oc * 128:(oc + 1) * 128, :]
            kb.emit("G", lambda e, slot=slot, src=src: e.dma_start(
                wbuf[slot][:], src),
                writes=[b_wbuf[slot]], dma=True)

        def store_for(oc):
            if oc in PC_K:
                j = oc - PC_K[0]
                return khat, j, b_khat[j], TT, True, "kv"
            if oc in PC_V:
                j = oc - PC_V[0]
                return vhat, j, b_vhat[j], TT, True, "kv"
            if oc in PC_Q:
                j = oc - PC_Q[0]
                return qhat, j, b_qhat[j], TOWN, True, "qx"
            if oc in PC_Z:
                j = oc - PC_Z[0]
                return zsilu, j, b_zsilu[j], TOWN, False, "z"
            return None, 0, b_bgraw, TT, False, "bg"

        def conv_slot(oc):
            return oc if oc < 12 else 12 + (oc - PC_Q[0])

        kb.emit("V", lambda e: e.memset(tails[:], 0.0), writes=[b_tails])

        def proj_pass(chunks, src_dram, tok0, ntok, qx_tails=False):
            nslab = ntok // SLAB
            for sl in range(nslab):
                xs_i = sl % 2
                xs, b_xs = xslab[xs_i], b_xslab[xs_i]
                kb.emit("G", lambda e, xs=xs, sl=sl, src_dram=src_dram: e.dma_start(
                    xs[:].rearrange("p (f t) -> p f t", t=SLAB),
                    src_dram.rearrange("(f p) t -> p f t", p=128)[:, :, sl * SLAB:(sl + 1) * SLAB]),
                    reads=[], writes=[b_xs], dma=True)
                for idx, oc in enumerate(chunks):
                    slot = idx % 2
                    if idx < 2:
                        wpanel_load(oc, slot)
                    dest, j, b_dest, ncols, do_conv, kind = store_for(oc)
                    ds = idx % 2
                    cs = conv_slot(oc) if do_conv else -1
                    if do_conv:
                        for tap in range(4):
                            kb.emit("V", lambda e, ds=ds, tap=tap, cs=cs: e.tensor_scalar(
                                dtile[ds][:, tap * 128:(tap + 1) * 128], cn["identb"][:],
                                cn["convw"][:, cs * 4 + tap:cs * 4 + tap + 1], None,
                                ALU.mult),
                                reads=[cb["identb"], cb["convw"], b_dtile[ds]],
                                writes=[b_dtile[ds]])
                        if qx_tails and sl == 0 and kind == "qx":
                            tp, b_tp = pool.alloc(4)
                            for f in range(16):
                                kb.emit("T", lambda e, f=f, tp=tp, slot=slot: e.matmul(
                                    tp, wbuf[slot][:, f * 128:(f + 1) * 128],
                                    tail3[:, f * 4:(f + 1) * 4],
                                    start=(f == 0), stop=(f == 15)),
                                    reads=[b_wbuf[slot], b_tail3], writes=[b_tp])
                            kb.emit("V", lambda e, cs=cs, tp=tp: e.tensor_copy(
                                tails[:, cs * 4:(cs + 1) * 4], tp),
                                reads=[b_tp], writes=[b_tails])
                    pt, b_pt = (poolS if (not do_conv and kind == "z") else pool).alloc(SLAB)
                    for f in range(16):
                        kb.emit("T", lambda e, f=f, pt=pt, slot=slot, xs=xs: e.matmul(
                            pt, wbuf[slot][:, f * 128:(f + 1) * 128],
                            xs[:, f * SLAB:(f + 1) * SLAB],
                            start=(f == 0), stop=(f == 15)),
                            reads=[b_wbuf[slot], b_xs], writes=[b_pt])
                    gtok = tok0 + sl * SLAB
                    dtok = gtok if kind in ("kv", "bg") else gtok - THALO
                    if kind == "bg":
                        kb.emit("V", lambda e, pt=pt, gtok=gtok: e.tensor_copy(
                            bgraw[:, gtok:gtok + SLAB], pt[0:32, :]),
                            reads=[b_pt], writes=[b_bgraw])
                    elif do_conv:
                        rs = idx % 2
                        stage, b_stage = raws[rs], b_raws[rs]
                        kb.emit("V", lambda e, stage=stage, cs=cs: e.tensor_copy(
                            stage[:, 0:4], tails[:, cs * 4:(cs + 1) * 4]),
                            reads=[b_tails], writes=[b_stage])
                        kb.emit("V", lambda e, stage=stage, pt=pt: e.tensor_copy(
                            stage[:, 4:4 + SLAB], pt), reads=[b_pt], writes=[b_stage])
                        kb.emit("V", lambda e, stage=stage, cs=cs: e.tensor_copy(
                            tails[:, cs * 4:(cs + 1) * 4], stage[:, SLAB:SLAB + 4]),
                            reads=[b_stage], writes=[b_tails])
                        pc, b_pc = poolS.alloc(SLAB)
                        for tap in range(4):
                            kb.emit("T", lambda e, ds=ds, tap=tap, pc=pc, stage=stage:
                                    e.matmul(pc, dtile[ds][:, tap * 128:(tap + 1) * 128],
                                             stage[:, 1 + tap:1 + tap + SLAB],
                                             start=(tap == 0), stop=(tap == 3)),
                                    reads=[b_dtile[ds], b_stage], writes=[b_pc])
                        kb.emit("S", lambda e, dest=dest, j=j, dtok=dtok, ncols=ncols,
                                pc=pc: e.activation(
                                    dest[:, j * ncols + dtok:j * ncols + dtok + SLAB],
                                    pc, AF.Silu),
                                reads=[b_pc], writes=[b_dest])
                    else:
                        kb.emit("S", lambda e, dest=dest, j=j, dtok=dtok, ncols=ncols,
                                pt=pt: e.activation(
                                    dest[:, j * ncols + dtok:j * ncols + dtok + SLAB],
                                    pt, AF.Silu),
                                reads=[b_pt], writes=[b_dest])
                    if idx + 2 < len(chunks):
                        wpanel_load(chunks[idx + 2], (idx + 2) % 2)

        proj_pass(HALO_PC, dr["xhalo"], 0, THALO)
        kb.emit("G", lambda e: e.dma_start(
            tail3[:].rearrange("p (f u) -> p f u", u=4),
            dr["xhalo"].rearrange("(f p) t -> p f t", p=128)[:, :, THALO - 4:THALO]),
            writes=[b_tail3], dma=True)
        proj_pass(PC_K + PC_V + PC_BG + PC_Q, dr["xin"], THALO, TOWN, qx_tails=True)

        kb.on = phase >= 2
        ssk = af(O_SSK, TT, 0, 16)
        ssq = af(O_SSQ, TOWN, 0, 16)
        b_ssk, b_ssq = Buf("ssk"), Buf("ssq")
        sqsc = [abf(O_SQSC + i * SLAB, SLAB) for i in range(2)]
        b_sqsc = [Buf(f"sqsc{i}") for i in range(2)]

        def sumsq(dest, b_dest, src, b_srcs, nj, ncols):
            for sl in range(ncols // SLAB):
                pt, b_pt = pool.alloc(SLAB)
                pt16 = pt[0:16, :]
                for j in range(nj):
                    kb.emit("S", lambda e, j=j, sl=sl: e.activation(
                        sqsc[j % 2][:],
                        src[:, j * ncols + sl * SLAB:j * ncols + (sl + 1) * SLAB],
                        AF.Square),
                        reads=[b_srcs[j]], writes=[b_sqsc[j % 2]])
                    kb.emit("T", lambda e, j=j, pt16=pt16: e.matmul(
                        pt16, bo16[:, j * 16:(j + 1) * 16], sqsc[j % 2][:],
                        start=(j == 0), stop=(j == nj - 1)),
                        reads=[cb["bo16"], b_sqsc[j % 2]], writes=[b_pt])
                kb.emit("V", lambda e, sl=sl, pt16=pt16: e.tensor_copy(
                    dest[:, sl * SLAB:(sl + 1) * SLAB], pt16),
                    reads=[b_pt], writes=[b_dest])

        sumsq(ssk, b_ssk, khat, b_khat, 4, TT)
        sumsq(ssq, b_ssq, qhat, b_qhat, 4, TOWN)
        for tn, tb, ncols, scale in [(ssk, b_ssk, TT, 1.0), (ssq, b_ssq, TOWN, 1.0 / 64.0)]:
            kb.emit("V", lambda e, tn=tn, ncols=ncols: e.tensor_scalar(
                tn[:, 0:ncols], tn[:, 0:ncols], EPS_L2, None, ALU.add),
                reads=[tb], writes=[tb])
            kb.emit("V", lambda e, tn=tn, ncols=ncols: e.reciprocal(
                tn[:, 0:ncols], tn[:, 0:ncols]), reads=[tb], writes=[tb])
            kb.emit("S", lambda e, tn=tn, ncols=ncols, scale=scale: e.activation(
                tn[:, 0:ncols], tn[:, 0:ncols], AF.Sqrt, scale=scale),
                reads=[tb], writes=[tb])

        def l2apply(dest, b_dests, rn, b_rn, nj, ncols):
            for j in range(nj):
                for sl in range(ncols // SLAB):
                    pt, b_pt = pool.alloc(SLAB)
                    kb.emit("T", lambda e, j=j, sl=sl, pt=pt: e.matmul(
                        pt, selhp[:, j * 128:(j + 1) * 128],
                        rn[:, sl * SLAB:(sl + 1) * SLAB],
                        start=True, stop=True),
                        reads=[cb["selhp"], b_rn], writes=[b_pt])
                    sl_ap = dest[:, j * ncols + sl * SLAB:j * ncols + (sl + 1) * SLAB]
                    kb.emit("V", lambda e, sl_ap=sl_ap, pt=pt: e.tensor_tensor(
                        sl_ap, sl_ap, pt, ALU.mult),
                        reads=[b_dests[j], b_pt], writes=[b_dests[j]])

        l2apply(khat, b_khat, ssk, b_ssk, 4, TT)
        l2apply(qhat, b_qhat, ssq, b_ssq, 4, TOWN)

        kb.on = phase >= 3
        b_hm = Buf("hm")
        ebc = af(O_MISC + 4, NCH_ALL, 0, 16)
        b_ebc = Buf("ebc")
        tmp16 = af(O_TMP16, TT, 0, 16)
        b_tmp16 = Buf("tmp16")
        negea = af(O_MISC, 1, 0, 16)
        b_negea = Buf("negea")
        inherit(b_hm, [b_ssq])

        def hma(a):
            return AFA[0:16, O_HM + a * TT:O_HM + (a + 1) * TT]

        kb.emit("S", lambda e: e.activation(negea[:], cn["alog"][:], AF.Exp),
                reads=[cb["alog"]], writes=[b_negea])
        kb.emit("V", lambda e: e.tensor_scalar(negea[:], negea[:], -1.0, None, ALU.mult),
                reads=[b_negea], writes=[b_negea])
        gpre16 = af(O_HM + A_G * TT, TT, 0, 16)
        b_gpre = Buf("gpre16")
        inherit(b_gpre, [b_ssq])
        kb.emit("G", lambda e: e.dma_start(gpre16, bgraw[16:32, :]),
                reads=[b_bgraw], writes=[b_gpre], dma=True)
        kb.emit("S", lambda e: e.activation(tmp16[:], gpre16, AF.Exp,
                                            bias=cn["dtb"][:]),
                reads=[b_gpre, cb["dtb"]], writes=[b_tmp16])
        kb.emit("S", lambda e: e.add(tmp16[:], tmp16[:], 1.0),
                reads=[b_tmp16], writes=[b_tmp16])
        kb.emit("S", lambda e: e.activation(tmp16[:], tmp16[:], AF.Ln),
                reads=[b_tmp16], writes=[b_tmp16])
        inherit(b_hm, [b_gpre])
        kb.emit("V", lambda e: e.tensor_scalar(hma(A_G), tmp16[:], negea[:], None, ALU.mult),
                reads=[b_tmp16, b_negea], writes=[b_hm])
        for c in range(NCH_ALL):
            kb.emit("V", lambda e, c=c: e.tensor_tensor_scan(
                hma(A_B)[:, c * CH:(c + 1) * CH], hma(A_G)[:, c * CH:(c + 1) * CH],
                hma(A_G)[:, c * CH:(c + 1) * CH], 0.0, ALU.add, ALU.bypass),
                reads=[b_hm], writes=[b_hm])
        kb.emit("S", lambda e: e.activation(tmp16[:], bgraw[0:16, :], AF.Exp, scale=-1.0),
                reads=[b_bgraw], writes=[b_tmp16])
        kb.emit("S", lambda e: e.add(tmp16[:], tmp16[:], 1.0),
                reads=[b_tmp16], writes=[b_tmp16])
        kb.emit("S", lambda e: e.activation(tmp16[:], tmp16[:], AF.Ln),
                reads=[b_tmp16], writes=[b_tmp16])
        kb.emit("V", lambda e: e.tensor_tensor(hma(A_BLOGB), hma(A_B), tmp16[:],
                                               ALU.subtract),
                reads=[b_hm, b_tmp16], writes=[b_hm])
        kb.emit("S", lambda e: e.activation(hma(A_EBLOGB), hma(A_BLOGB), AF.Exp),
                reads=[b_hm], writes=[b_hm])
        kb.emit("S", lambda e: e.activation(hma(A_BETA), bgraw[0:16, :], AF.Sigmoid),
                reads=[b_bgraw], writes=[b_hm])
        for c in range(NCH_ALL):
            kb.emit("V", lambda e, c=c: e.tensor_scalar(
                hma(A_EBAR)[:, c * CH:(c + 1) * CH], hma(A_B)[:, c * CH:(c + 1) * CH],
                hma(A_B)[:, c * CH + CH - 1:c * CH + CH], None, ALU.subtract),
                reads=[b_hm], writes=[b_hm])
            kb.emit("V", lambda e, c=c: e.tensor_copy(
                ebc[:, c:c + 1], hma(A_B)[:, c * CH + CH - 1:c * CH + CH]),
                reads=[b_hm], writes=[b_ebc])
        kb.emit("S", lambda e: e.activation(hma(A_EBAR), hma(A_EBAR), AF.Exp, scale=-1.0),
                reads=[b_hm], writes=[b_hm])
        kb.emit("S", lambda e: e.activation(ebc[:], ebc[:], AF.Exp),
                reads=[b_ebc], writes=[b_ebc])

        hmP = af(O_HMP, 2 * TT, 0, 16)
        b_hmP = Buf("hmP")

        kb.on = phase >= 4
        ktm = abf(O_KTM, 16 * 512)
        vtm = abf(O_VTM, 16 * 1024)
        b_ktm, b_vtm = Buf("ktm"), Buf("vtm")
        inherit(b_ktm, b_wbuf + b_raws + b_dtile + b_sqsc)
        inherit(b_vtm, b_xslab)

        def transpose_bf(src_ap, b_srcs, dst_ap, b_dst):
            tp, b_tp = poolb.alloc(128)
            kb.emit("T", lambda e, tp=tp, src_ap=src_ap: e.transpose(
                tp, src_ap, cn["identb"][:]),
                reads=list(b_srcs) + [cb["identb"]], writes=[b_tp])
            kb.emit("V", lambda e, tp=tp, dst_ap=dst_ap: e.tensor_copy(dst_ap, tp),
                    reads=[b_tp], writes=[b_dst])

        for blk in range(16):
            for j in range(4):
                transpose_bf(khat[:, j * TT + blk * 128:j * TT + (blk + 1) * 128],
                             [b_khat[j]],
                             ktm[:, blk * 512 + j * 128:blk * 512 + (j + 1) * 128], b_ktm)
            for j in range(8):
                transpose_bf(vhat[:, j * TT + blk * 128:j * TT + (blk + 1) * 128],
                             [b_vhat[j]],
                             vtm[:, blk * 1024 + j * 128:blk * 1024 + (j + 1) * 128], b_vtm)

        xint = af(O_XINT, 2 * TT, 0, 40)
        b_xint = Buf("xint")
        inherit(b_xint, [b_tmp16])
        inherit(b_xint, [b_bgraw])
        xcol = af(O_XCOL, NCH_ALL * 40)
        b_xcol = Buf("xcol")
        for ai, arr in enumerate(XARR):
            for par in range(2):
                kb.emit("G", lambda e, ai=ai, arr=arr, par=par: e.dma_start(
                    xint[ai * 8:ai * 8 + 8, :].rearrange("p (c u) -> p c u", u=128)[:, :, par * 64:par * 64 + 64],
                    hma(arr)[par:16:2, :].rearrange("p (c u) -> p c u", u=64)),
                    reads=[b_hm], writes=[b_xint], dma=True)
        for c in range(NCH_ALL):
            tp, b_tp = pool.alloc(40)
            kb.emit("T", lambda e, c=c, tp=tp: e.transpose(
                tp, xint[:, c * 128:(c + 1) * 128], cn["identf"][0:40, 0:40]),
                reads=[b_xint, cb["identf"]], writes=[b_tp])
            kb.emit("V", lambda e, c=c, tp=tp: e.tensor_copy(
                xcol[:, c * 40:(c + 1) * 40], tp), reads=[b_tp], writes=[b_xcol])
            kb.emit("V", lambda e, c=c: e.tensor_scalar(
                xcol[:, c * 40:c * 40 + 8], xcol[:, c * 40:c * 40 + 8], -1.0, None,
                ALU.mult), reads=[b_xcol], writes=[b_xcol])

        def XC(xi, c, hp):
            return xcol[:, c * 40 + xi * 8 + hp:c * 40 + xi * 8 + hp + 1]

        inherit(b_hmP, [b_xint])
        for ai, arr in enumerate([A_BLOGB, A_B]):
            kb.emit("V", lambda e, ai=ai, arr=arr: e.tensor_copy(
                hmP[:, :].rearrange("p (c u) -> p c u", u=128)[:, :, ai * 64:ai * 64 + 64],
                hma(arr).rearrange("p (c u) -> p c u", u=64)),
                reads=[b_hm], writes=[b_hmP])

        ebchp = af(O_EBCHP, 8 * 32)
        b_ebchp = Buf("ebchp")
        for hp in range(8):
            tp, b_tp = pool.alloc(32)
            kb.emit("T", lambda e, hp=hp, tp=tp: e.matmul(
                tp, selhp[:, hp * 128:(hp + 1) * 128], ebc[:, :],
                start=True, stop=True),
                reads=[cb["selhp"], b_ebc], writes=[b_tp])
            kb.emit("V", lambda e, hp=hp, tp=tp: e.tensor_copy(
                ebchp[:, hp * 32:(hp + 1) * 32], tp), reads=[b_tp], writes=[b_ebchp])

        def EBCC(c, hp):
            return ebchp[:, hp * 32 + c:hp * 32 + c + 1]

        MACH_SIZES = [("wNT", 128), ("wNP", 128), ("wR", 256), ("wGL", 64),
                      ("wDT", 64), ("wKB", 64), ("wPT", 64), ("wQE", 64),
                      ("exmNT", 64), ("exmNP", 64), ("exmFTI", 64)]
        mt, moff = {}, O_MACH
        for nm, width in MACH_SIZES:
            mt[nm] = [abf(moff + h * 960, width) for h in range(8)]
        off2 = 0
        for nm, width in MACH_SIZES:
            for h in range(8):
                mt[nm][h] = abf(O_MACH + h * 960 + off2, width)
            off2 += width
        wNT, wNP, wR = mt["wNT"], mt["wNP"], mt["wR"]
        wGL, wDT = mt["wGL"], mt["wDT"]
        wKB, wPT, wQE = mt["wKB"], mt["wPT"], mt["wQE"]
        exmNT, exmNP, exmFTI = mt["exmNT"], mt["exmNP"], mt["exmFTI"]
        b_wNT = [Buf(f"wNT{h}") for h in range(8)]
        b_wNP = [Buf(f"wNP{h}") for h in range(8)]
        b_wR = [Buf(f"wR{h}") for h in range(8)]
        b_wGL = [Buf(f"wGL{h}") for h in range(8)]
        b_wDT = [Buf(f"wDT{h}") for h in range(8)]
        b_wKB = [Buf(f"wKB{h}") for h in range(8)]
        b_wPT = [Buf(f"wPT{h}") for h in range(8)]
        b_wQE = [Buf(f"wQE{h}") for h in range(8)]
        b_exmNT = [Buf(f"exmNT{h}") for h in range(8)]
        b_exmNP = [Buf(f"exmNP{h}") for h in range(8)]
        b_exmFTI = [Buf(f"exmFTI{h}") for h in range(8)]
        wEx = [af(O_WEX + h * 256, 256) for h in range(8)]
        b_wEx = [Buf(f"wEx{h}") for h in range(8)]
        wE1 = [w[:, 192:256] for w in wEx]
        b_wE1 = b_wEx
        Sst = [af(O_SST + h * 128, 128) for h in range(8)]
        b_Sst = [Buf(f"Sst{h}") for h in range(8)]
        Sb16t = sb("Sb16", 128, 8 * 64, BF16)
        Sb16 = [Sb16t[:, h * 64:(h + 1) * 64] for h in range(8)]
        b_Sb16 = [Buf(f"Sb16{h}") for h in range(8)]
        ohat = abf(O_OHAT, 8 * TOWN)
        b_ohat = [Buf(f"ohat{h}") for h in range(8)]
        mach_bufs = (b_wNT + b_wNP + b_wR + b_wGL + b_wDT
                     + b_wKB + b_wPT + b_wQE + b_exmNT + b_exmNP + b_exmFTI)
        for bb in mach_bufs + b_ohat:
            inherit(bb, [b_vtm if False else b_vhat[0]])
            for bv in b_vhat:
                inherit(bb, bv)
        for bb in b_wEx + b_Sst:
            inherit(bb, [b_hm])
        for h in range(8):
            kb.emit("V", lambda e, h=h: e.memset(Sst[h][:, 0:64], 0.0),
                    writes=[b_Sst[h]])
            kb.emit("V", lambda e, h=h: e.memset(Sb16[h], 0.0), writes=[b_Sb16[h]])
            kb.emit("V", lambda e, h=h: e.memset(exmNT[h][:], 0.0), writes=[b_exmNT[h]])
            kb.emit("V", lambda e, h=h: e.memset(exmNP[h][:], 0.0), writes=[b_exmNP[h]])
            kb.emit("V", lambda e, h=h: e.memset(exmFTI[h][:], 0.0),
                    writes=[b_exmFTI[h]])

        poolMV = PPool([(pp, 0, 512, b_pp0), (pp, 512, 512, b_pp1)])
        poolMS = PPool([(pm, 0, 512, b_pm), (pmbf, 0, 512, b_pmb)])

        def mach_chunk(c):
            parts = DBG["mach_parts"]
            is_own = c >= NCH_ALL - NCH_OWN
            co = c - (NCH_ALL - NCH_OWN)
            blk, par = c // 2, c % 2
            rr = c % 2
            if not parts & 1:
                return
            for wv in range(2):
                hps = list(range(wv * 4, wv * 4 + 4))
                slots = {}
                for hp in hps:
                    koff = (hp // 2) * TT + c * CH
                    klo = (hp % 2) * 64
                    kap = khat[klo:klo + 64, koff:koff + CH]
                    gp, b_gp = poolMV.alloc(64)
                    for half in range(2):
                        kb.emit("T", lambda e, kap=kap, gp=gp, half=half, klo=klo: e.matmul(
                            gp[half * 64:half * 64 + 64, :], kap, kap, start=True, stop=True,
                            tile_position=(klo, half * 64)),
                            reads=[b_khat[hp // 2]], writes=[b_gp])
                    p1, b_p1 = poolMS.alloc(128)
                    kb.emit("T", lambda e, hp=hp, p1=p1, c=c: e.matmul(
                        p1, selhp[:, hp * 128:(hp + 1) * 128],
                        hmP[:, c * 128:(c + 1) * 128],
                        start=True, stop=True),
                        reads=[cb["selhp"], b_hmP], writes=[b_p1])
                    kp, b_kp = poolMV.alloc(64)
                    kb.emit("T", lambda e, kp=kp, blk=blk, par=par, hp=hp: e.matmul(
                        kp, permd2[:, par * 128:(par + 1) * 128],
                        ktm[:, blk * 512 + (hp // 2) * 128 + (hp % 2) * 64:
                            blk * 512 + (hp // 2) * 128 + (hp % 2) * 64 + 64],
                        start=True, stop=True),
                        reads=[cb["permd2"], b_ktm], writes=[b_kp])
                    vp, b_vp = poolMV.alloc(64)
                    for half in range(2):
                        kb.emit("T", lambda e, vp=vp, blk=blk, par=par, hp=hp, half=half:
                                e.matmul(vp[half * 64:half * 64 + 64, :],
                                         permh[:, par * 64:(par + 1) * 64],
                                         vtm[:, blk * 1024 + hp * 128 + half * 64:
                                             blk * 1024 + hp * 128 + half * 64 + 64],
                                         start=True, stop=True, tile_position=(0, half * 64)),
                                reads=[cb["permh"], b_vtm], writes=[b_vp])
                    dp = b_dp = None
                    if is_own:
                        qoff = (hp // 2) * TOWN + co * CH
                        qap = qhat[klo:klo + 64, qoff:qoff + CH]
                        dp, b_dp = poolMV.alloc(64)
                        for half in range(2):
                            kb.emit("T", lambda e, kap=kap, qap=qap, dp=dp, half=half, klo=klo:
                                    e.matmul(dp[half * 64:half * 64 + 64, :], kap, qap,
                                             start=True, stop=True,
                                             tile_position=(klo, half * 64)),
                                    reads=[b_khat[hp // 2], b_qhat[hp // 2]], writes=[b_dp])
                    slots[hp] = (gp, b_gp, p1, b_p1, kp, b_kp, vp, b_vp, dp, b_dp)
                for hp in hps:
                    gp, b_gp, p1, b_p1, kp, b_kp, vp, b_vp, dp, b_dp = slots[hp]
                    ex, b_ex = wEx[hp], b_wEx[hp]
                    kb.emit("S", lambda e, ex=ex, p1=p1, c=c, hp=hp: e.activation(
                        ex[:, 0:64], p1[:, 0:64], AF.Exp, bias=XC(XI_NEGB, c, hp)),
                        reads=[b_p1, b_xcol], writes=[b_ex])
                    kb.emit("S", lambda e, ex=ex, p1=p1, c=c, hp=hp: e.activation(
                        ex[:, 64:128], p1[:, 64:128], AF.Exp, scale=-1.0,
                        bias=XC(XI_BLOGB, c, hp)),
                        reads=[b_p1, b_xcol], writes=[b_ex])
                    if is_own:
                        kb.emit("S", lambda e, ex=ex, p1=p1, c=c, hp=hp: e.activation(
                            ex[:, 128:192], p1[:, 64:128], AF.Exp, bias=XC(XI_NEGB, c, hp)),
                            reads=[b_p1, b_xcol], writes=[b_ex])
                        kb.emit("S", lambda e, p1=p1, hp=hp: e.activation(
                            wE1[hp][:], p1[:, 64:128], AF.Exp),
                            reads=[b_p1], writes=[b_wE1[hp]])
                for hp in hps:
                    gp, b_gp, p1, b_p1, kp, b_kp, vp, b_vp, dp, b_dp = slots[hp]
                    ex, b_ex = wEx[hp], b_wEx[hp]
                    kb.emit("V", lambda e, hp=hp, ex=ex: e.copy_predicated(
                        exmNT[hp][:], cn["mus"][:], ex[:, 0:64]),
                        reads=[cb["mus"], b_ex], writes=[b_exmNT[hp]])
                    kb.emit("V", lambda e, hp=hp, ex=ex: e.copy_predicated(
                        exmNP[hp][:], cn["mls"][:], ex[:, 64:128]),
                        reads=[cb["mls"], b_ex], writes=[b_exmNP[hp]])
                    kb.emit("V", lambda e, gp=gp, hp=hp, rr=rr: e.scalar_tensor_tensor(
                        wNT[hp][:, rr * 64:rr * 64 + 64], exmNT[hp][:], -1.0, gp,
                        ALU.mult, ALU.mult),
                        reads=[b_exmNT[hp], b_gp], writes=[b_wNT[hp]])
                    kb.emit("V", lambda e, gp=gp, hp=hp, rr=rr: e.scalar_tensor_tensor(
                        wNP[hp][:, rr * 64:rr * 64 + 64], exmNP[hp][:], -1.0, gp,
                        ALU.mult, ALU.mult),
                        reads=[b_exmNP[hp], b_gp], writes=[b_wNP[hp]])
                    if is_own:
                        kb.emit("V", lambda e, hp=hp, ex=ex: e.copy_predicated(
                            exmFTI[hp][:], cn["mui"][:], ex[:, 128:192]),
                            reads=[cb["mui"], b_ex], writes=[b_exmFTI[hp]])
                        kb.emit("V", lambda e, dp=dp, hp=hp: e.tensor_tensor(
                            wDT[hp][:], exmFTI[hp][:], dp, ALU.mult),
                            reads=[b_exmFTI[hp], b_dp], writes=[b_wDT[hp]])
                    kb.emit("V", lambda e, hp=hp, kp=kp, c=c, rr=rr: e.tensor_scalar(
                        wR[hp][:, rr * 128:rr * 128 + 64], kp, XC(XI_EBLOGB, c, hp), None,
                        ALU.mult),
                        reads=[b_kp, b_xcol], writes=[b_wR[hp]])
                    kb.emit("V", lambda e, hp=hp, vp=vp, c=c, rr=rr: e.tensor_scalar(
                        wR[hp][:, rr * 128 + 64:rr * 128 + 128], vp, XC(XI_BETA, c, hp), None,
                        ALU.mult),
                        reads=[b_vp, b_xcol], writes=[b_wR[hp]])
                    kb.emit("V", lambda e, hp=hp, kp=kp, c=c: e.tensor_scalar(
                        wKB[hp][:], kp, XC(XI_EBAR, c, hp), None, ALU.mult),
                        reads=[b_kp, b_xcol], writes=[b_wKB[hp]])
            cur = rr
            if not parts & 2:
                return
            for j in range(NLEV):
                nxt = 1 - cur
                slots = {}
                for hp in range(8):
                    cp_ = poolMV if hp % 2 == 0 else poolMS
                    rp, b_rp = cp_.alloc(128)
                    kb.emit("T", lambda e, hp=hp, rp=rp, cur=cur: e.matmul(
                        rp, cn["identb"][:, :],
                        wR[hp][:, cur * 128:cur * 128 + 128],
                        start=True, stop=False),
                        reads=[cb["identb"], b_wR[hp]], writes=[b_rp])
                    for half in range(2):
                        lo = half * 64
                        kb.emit("T", lambda e, hp=hp, lo=lo, rp=rp, cur=cur: e.matmul(
                            rp[lo:lo + 64, :], wNT[hp][lo:lo + 64, cur * 64:cur * 64 + 64],
                            wR[hp][lo:lo + 64, cur * 128:cur * 128 + 128],
                            start=False, stop=(half == 1), tile_position=(lo, lo)),
                            reads=[b_wNT[hp], b_wR[hp]], writes=[b_rp])
                    sq1 = sq2 = b_sq1 = b_sq2 = None
                    if j < NLEV - 1:
                        sq1, b_sq1 = cp_.alloc(64)
                        if j < NLEV - 2:
                            sq2, b_sq2 = cp_.alloc(64)
                        for half in range(2):
                            lo = half * 64
                            kb.emit("T", lambda e, hp=hp, lo=lo, sq1=sq1, cur=cur: e.matmul(
                                sq1[lo:lo + 64, :],
                                wNP[hp][lo:lo + 64, cur * 64:cur * 64 + 64],
                                wNT[hp][lo:lo + 64, cur * 64:cur * 64 + 64],
                                start=True, stop=True, tile_position=(lo, lo)),
                                reads=[b_wNP[hp], b_wNT[hp]], writes=[b_sq1])
                            if sq2 is not None:
                                kb.emit("T", lambda e, hp=hp, lo=lo, sq2=sq2, cur=cur:
                                        e.matmul(
                                            sq2[lo:lo + 64, :],
                                            wNT[hp][lo:lo + 64, cur * 64:cur * 64 + 64],
                                            wNP[hp][lo:lo + 64, cur * 64:cur * 64 + 64],
                                            start=True, stop=True, tile_position=(lo, lo)),
                                        reads=[b_wNP[hp], b_wNT[hp]], writes=[b_sq2])
                    slots[hp] = (rp, b_rp, sq1, b_sq1, sq2, b_sq2)
                for hp in range(8):
                    rp, b_rp, sq1, b_sq1, sq2, b_sq2 = slots[hp]
                    if hp % 2 == 0:
                        kb.emit("V", lambda e, hp=hp, rp=rp, nxt=nxt: e.tensor_copy(
                            wR[hp][:, nxt * 128:nxt * 128 + 128], rp),
                            reads=[b_rp], writes=[b_wR[hp]])
                        if sq1 is not None:
                            kb.emit("V", lambda e, hp=hp, sq1=sq1, nxt=nxt: e.tensor_copy(
                                wNT[hp][:, nxt * 64:nxt * 64 + 64], sq1),
                                reads=[b_sq1], writes=[b_wNT[hp]])
                        if sq2 is not None:
                            kb.emit("V", lambda e, hp=hp, sq2=sq2, nxt=nxt: e.tensor_copy(
                                wNP[hp][:, nxt * 64:nxt * 64 + 64], sq2),
                                reads=[b_sq2], writes=[b_wNP[hp]])
                    else:
                        kb.emit("S", lambda e, hp=hp, rp=rp, nxt=nxt: e.copy(
                            wR[hp][:, nxt * 128:nxt * 128 + 128], rp),
                            reads=[b_rp], writes=[b_wR[hp]])
                        if sq1 is not None:
                            kb.emit("S", lambda e, hp=hp, sq1=sq1, nxt=nxt: e.copy(
                                wNT[hp][:, nxt * 64:nxt * 64 + 64], sq1),
                                reads=[b_sq1], writes=[b_wNT[hp]])
                        if sq2 is not None:
                            kb.emit("S", lambda e, hp=hp, sq2=sq2, nxt=nxt: e.copy(
                                wNP[hp][:, nxt * 64:nxt * 64 + 64], sq2),
                                reads=[b_sq2], writes=[b_wNP[hp]])
                cur = nxt
            rfin = cur
            if not parts & 4:
                return
            for hp in range(8):
                Y1 = wR[hp][:, rfin * 128:rfin * 128 + 64]
                pt_, b_pt = poolMV.alloc(64)
                for half in range(2):
                    lo = half * 64
                    kb.emit("T", lambda e, hp=hp, lo=lo, pt_=pt_, Y1=Y1: e.matmul(
                        pt_[lo:lo + 64, :], Y1[lo:lo + 64, :], wKB[hp][lo:lo + 64, :],
                        start=True, stop=True, tile_position=(lo, lo)),
                        reads=[b_wR[hp], b_wKB[hp]], writes=[b_pt])
                kb.emit("V", lambda e, hp=hp, pt_=pt_, c=c: e.scalar_tensor_tensor(
                    wPT[hp][:], cn["is2"][:], EBCC(c, hp), pt_, ALU.mult, ALU.subtract),
                    reads=[cb["is2"], b_ebchp, b_pt], writes=[b_wPT[hp]])
                if is_own:
                    qe_, b_qe = poolMV.alloc(64)
                    for half in range(2):
                        lo = half * 64
                        kb.emit("T", lambda e, hp=hp, lo=lo, qe_=qe_, Y1=Y1: e.matmul(
                            qe_[lo:lo + 64, :], Y1[lo:lo + 64, :], wDT[hp][lo:lo + 64, :],
                            start=True, stop=True, tile_position=(lo, lo)),
                            reads=[b_wR[hp], b_wDT[hp]], writes=[b_qe])
                    klo = (hp % 2) * 64
                    qoff = (hp // 2) * TOWN + co * CH
                    qap = qhat[klo:klo + 64, qoff:qoff + CH]
                    qd, b_qd = poolMV.alloc(64)
                    for half in range(2):
                        kb.emit("T", lambda e, qap=qap, qd=qd, half=half, klo=klo:
                                e.matmul(qd[half * 64:half * 64 + 64, :],
                                         cn["identb"][klo:klo + 64, klo:klo + 64], qap,
                                         start=True, stop=True,
                                         tile_position=(klo, half * 64)),
                                reads=[cb["identb"], b_qhat[hp // 2]], writes=[b_qd])
                    kb.emit("V", lambda e, hp=hp, qd=qd: e.tensor_tensor(
                        wGL[hp][:], qd, wE1[hp][:], ALU.mult),
                        reads=[b_qd, b_wE1[hp]], writes=[b_wGL[hp]])
                    kb.emit("V", lambda e, hp=hp, qe_=qe_: e.tensor_tensor(
                        wQE[hp][:], wGL[hp][:], qe_, ALU.subtract),
                        reads=[b_wGL[hp], b_qe], writes=[b_wQE[hp]])
            if not parts & 8:
                return
            for hp in range(8):
                sring = psA if hp % 2 == 0 else psB
                b_sring = b_psA if hp % 2 == 0 else b_psB
                oring = poA if hp % 2 == 0 else poB
                b_oring = b_poA if hp % 2 == 0 else b_poB
                so = (hp // 2) * 64
                Y2 = wR[hp][:, rfin * 128 + 64:rfin * 128 + 128]
                sb16 = Sb16[hp]
                for half in range(2):
                    lo = half * 64
                    kb.emit("T", lambda e, hp=hp, lo=lo, so=so, Y2=Y2, sring=sring:
                            e.matmul(sring[lo:lo + 64, so:so + 64],
                                     wKB[hp][lo:lo + 64, :], Y2[lo:lo + 64, :],
                                     start=True, stop=False, tile_position=(lo, lo)),
                            reads=[b_wKB[hp], b_wR[hp]], writes=[b_sring])
                    kb.emit("T", lambda e, hp=hp, lo=lo, so=so, sb16=sb16, sring=sring:
                            e.matmul(sring[lo:lo + 64, so:so + 64],
                                     wPT[hp][lo:lo + 64, :], sb16[lo:lo + 64, :],
                                     start=False, stop=True, tile_position=(lo, lo)),
                            reads=[b_wPT[hp], b_Sb16[hp]], writes=[b_sring])
                if is_own:
                    for half in range(2):
                        lo = half * 64
                        kb.emit("T", lambda e, hp=hp, lo=lo, so=so, Y2=Y2, oring=oring:
                                e.matmul(oring[lo:lo + 64, so:so + 64],
                                         Y2[lo:lo + 64, :], wDT[hp][lo:lo + 64, :],
                                         start=True, stop=False, tile_position=(lo, lo)),
                                reads=[b_wR[hp], b_wDT[hp]], writes=[b_oring])
                        kb.emit("T", lambda e, hp=hp, lo=lo, so=so, sb16=sb16, oring=oring:
                                e.matmul(oring[lo:lo + 64, so:so + 64],
                                         sb16[lo:lo + 64, :], wQE[hp][lo:lo + 64, :],
                                         start=False, stop=True, tile_position=(lo, lo)),
                                reads=[b_Sb16[hp], b_wQE[hp]], writes=[b_oring])
            for hp in range(8):
                sring = psA if hp % 2 == 0 else psB
                b_sring = b_psA if hp % 2 == 0 else b_psB
                oring = poA if hp % 2 == 0 else poB
                b_oring = b_poA if hp % 2 == 0 else b_poB
                so = (hp // 2) * 64
                sold = Sst[hp][:, (c % 2) * 64:(c % 2) * 64 + 64]
                snew = Sst[hp][:, (1 - c % 2) * 64:(1 - c % 2) * 64 + 64]
                sb16 = Sb16[hp]
                kb.emit("V", lambda e, hp=hp, so=so, sold=sold, snew=snew, c=c,
                        sring=sring: e.scalar_tensor_tensor(
                            snew, sold, EBCC(c, hp), sring[:, so:so + 64],
                            ALU.mult, ALU.add),
                        reads=[b_Sst[hp], b_ebchp, b_sring], writes=[b_Sst[hp]])
                kb.emit("V", lambda e, hp=hp, snew=snew, sb16=sb16: e.tensor_copy(
                    sb16, snew), reads=[b_Sst[hp]], writes=[b_Sb16[hp]])
                if is_own:
                    kb.emit("S", lambda e, hp=hp, so=so, co=co, oring=oring: e.copy(
                        ohat[:, hp * TOWN + co * CH:hp * TOWN + (co + 1) * CH],
                        oring[:, so:so + 64]),
                        reads=[b_oring], writes=[b_ohat[hp]])


        kb.on = phase >= 5
        w0cast = nc.dram_tensor("w0cast", [64, 128, 16 * 128], BF16,
                                kind="Internal").ap()
        w1cast = nc.dram_tensor("w1cast", [64, 4, 128, 512], BF16,
                                kind="Internal").ap()
        wocast = nc.dram_tensor("wocast", [16, 128, 8 * 128], BF16,
                                kind="Internal").ap()
        wpcast = nc.dram_tensor("wpcast", [8, 128, 16 * 128], BF16,
                                kind="Internal").ap()
        b_w0c, b_w1c = Buf("w0c"), Buf("w1c")
        b_woc, b_wpc = Buf("woc"), Buf("wpc")
        for oc in range(16):
            kb.emit("G", lambda e, oc=oc: e.dma_start(
                wocast[oc].rearrange("p (f c) -> p f c", c=128),
                dr["woutT"][:, oc * 128:(oc + 1) * 128]
                .rearrange("(f p) c -> p f c", p=128)),
                writes=[b_woc], dma=True)
        for j in range(8):
            oc = PC_Z[0] + j
            kb.emit("G", lambda e, j=j, oc=oc: e.dma_start(
                wpcast[j], dr["wproj"][oc * 128:(oc + 1) * 128, :]),
                writes=[b_wpc], dma=True)
        for nb in range(64):
            kb.emit("G", lambda e, nb=nb: e.dma_start(
                w0cast[nb].rearrange("p (f c) -> p f c", c=128),
                dr["w0T"][:, nb * 128:(nb + 1) * 128]
                .rearrange("(f p) c -> p f c", p=128)),
                writes=[b_w0c], dma=True)
        for np_ in range(64):
            kb.emit("G", lambda e, np_=np_: e.dma_start(
                w1cast[np_], dr["w1T"][np_ * 128:(np_ + 1) * 128, :]
                .rearrange("p (o c) -> o p c", c=512)),
                writes=[b_w1c], dma=True)
        for c in range(min(DBG["mach_n"], NCH_ALL)):
            mach_chunk(c)

        kb.on = phase >= 6
        for i in range(2):
            inherit(b_wbuf[i], [b_ktm])
            inherit(b_xslab[i], [b_vtm])
        for bz in b_zsilu:
            inherit(bz, mach_bufs)
        proj_pass(PC_Z, dr["xin"], THALO, TOWN)

        ssO = af(O_SSO, TOWN, 0, 16)
        b_ssO = Buf("ssO")
        inherit(b_ssO, [b_xcol])
        for bb in b_sqsc:
            inherit(bb, [b_ktm])
        for sl in range(TOWN // SLAB):
            pt, b_pt = pool.alloc(SLAB)
            pt16 = pt[0:16, :]
            for hp in range(8):
                kb.emit("S", lambda e, hp=hp, sl=sl: e.activation(
                    sqsc[hp % 2][:],
                    ohat[:, hp * TOWN + sl * SLAB:hp * TOWN + (sl + 1) * SLAB], AF.Square),
                    reads=[b_ohat[hp]], writes=[b_sqsc[hp % 2]])
                kb.emit("T", lambda e, hp=hp, pt16=pt16: e.matmul(
                    pt16, bo16[:, hp * 16:(hp + 1) * 16], sqsc[hp % 2][:],
                    start=(hp == 0), stop=(hp == 7)),
                    reads=[cb["bo16"], b_sqsc[hp % 2]], writes=[b_pt])
            kb.emit("V", lambda e, sl=sl, pt16=pt16: e.tensor_copy(
                ssO[:, sl * SLAB:(sl + 1) * SLAB], pt16), reads=[b_pt], writes=[b_ssO])
        kb.emit("V", lambda e: e.tensor_scalar(ssO[:], ssO[:], 1.0 / 64.0, EPS_L2,
                                               ALU.mult, ALU.add),
                reads=[b_ssO], writes=[b_ssO])
        kb.emit("V", lambda e: e.reciprocal(ssO[:], ssO[:]), reads=[b_ssO], writes=[b_ssO])
        kb.emit("S", lambda e: e.activation(ssO[:], ssO[:], AF.Sqrt),
                reads=[b_ssO], writes=[b_ssO])
        nw1 = af(O_MISC + 2, 1)
        b_nw1 = Buf("nw1")
        kb.emit("V", lambda e: e.tensor_scalar(nw1[:], cn["nw2"][:], 1.0, None, ALU.add),
                reads=[cb["nw2"]], writes=[b_nw1])
        omov = abf(O_OMOV, 8 * TOWN)
        b_omov = [Buf(f"omov{h}") for h in range(8)]
        for bb in b_omov:
            inherit(bb, b_xslab + [b_vtm])
        for hp in range(8):
            for sl in range(TOWN // SLAB):
                pt, b_pt = pool.alloc(SLAB)
                kb.emit("T", lambda e, hp=hp, sl=sl, pt=pt: e.matmul(
                    pt, selhp[:, hp * 128:(hp + 1) * 128],
                    ssO[:, sl * SLAB:(sl + 1) * SLAB],
                    start=True, stop=True),
                    reads=[cb["selhp"], b_ssO], writes=[b_pt])
                off = hp * TOWN + sl * SLAB
                kb.emit("V", lambda e, hp=hp, off=off, pt=pt: e.scalar_tensor_tensor(
                    sqsc[0][:], ohat[:, off:off + SLAB], nw1[:], pt, ALU.mult, ALU.mult),
                    reads=[b_ohat[hp], b_nw1, b_pt], writes=[b_sqsc[0]])
                kb.emit("V", lambda e, hp=hp, off=off: e.tensor_tensor(
                    omov[:, off:off + SLAB], sqsc[0][:], zsilu[:, off:off + SLAB],
                    ALU.mult),
                    reads=[b_sqsc[0], b_zsilu[hp]], writes=[b_omov[hp]])

        xin = af(O_XIN, 16 * TOWN)
        b_xin = [Buf(f"xin{j}") for j in range(16)]
        for bb in b_xin:
            inherit(bb, [b_hmP] + b_wEx + b_Sst + [b_ssk, b_hm, b_tmp16])
        for j in range(16):
            kb.emit("Q", lambda e, j=j: e.dma_start(
                xin[:, j * TOWN:(j + 1) * TOWN],
                dr["xin"].rearrange("(f p) t -> p f t", p=128)[:, j, :]),
                writes=[b_xin[j]], dma=True)

        wobuf = [abf(O_WOBUF + i * 1024, 1024) for i in range(2)]
        b_wobuf = [Buf(f"wobuf{i}") for i in range(2)]
        for bb in b_wobuf:
            for bk in b_khat:
                inherit(bb, bk)

        def wopanel(oc, slot):
            src = wocast[oc]
            kb.emit("Q", lambda e, slot=slot, src=src: e.dma_start(
                wobuf[slot][:], src),
                reads=[b_woc], writes=[b_wobuf[slot]], dma=True)

        wopanel(0, 0)
        wopanel(1, 1)
        for oc in range(16):
            slot = oc % 2
            for sl in range(TOWN // SLAB):
                pt, b_pt = pool.alloc(SLAB)
                for vc in range(8):
                    kb.emit("T", lambda e, oc=oc, sl=sl, vc=vc, pt=pt, slot=slot: e.matmul(
                        pt, wobuf[slot][:, vc * 128:(vc + 1) * 128],
                        omov[:, vc * TOWN + sl * SLAB:vc * TOWN + (sl + 1) * SLAB],
                        start=(vc == 0), stop=(vc == 7)),
                        reads=[b_wobuf[slot], b_omov[vc]], writes=[b_pt])
                kb.emit("V", lambda e, oc=oc, sl=sl, pt=pt: e.tensor_tensor(
                    xin[:, oc * TOWN + sl * SLAB:oc * TOWN + (sl + 1) * SLAB],
                    xin[:, oc * TOWN + sl * SLAB:oc * TOWN + (sl + 1) * SLAB], pt,
                    ALU.add),
                    reads=[b_xin[oc], b_pt], writes=[b_xin[oc]])
            if oc + 2 < 16:
                wopanel(oc + 2, slot)

        y1b = abf(O_Y1B, 16 * TOWN)
        b_y1b = [Buf(f"y1b{j}") for j in range(16)]
        for bb in b_y1b:
            inherit(bb, [b_vtm] + b_omov + b_xslab)
        LNR = {"m": O_LNM, "ms": O_LNR, "r": O_LNR, "t1": O_LNT, "mr": O_LNT}

        def lnr(nm, sl=None):
            c0 = LNR[nm]
            if sl is None:
                return AFA[0:1, c0:c0 + TOWN]
            return AFA[0:1, c0 + sl * SLAB:c0 + (sl + 1) * SLAB]
        b_lnrow = Buf("lnrow")
        inherit(b_lnrow, [b_ssO, b_ebchp])

        def layernorm(make_bf16):
            for j in range(16):
                kb.emit("V", lambda e, j=j: e.tensor_copy(
                    y1b[:, j * TOWN:(j + 1) * TOWN], xin[:, j * TOWN:(j + 1) * TOWN]),
                    reads=[b_xin[j]], writes=[b_y1b[j]])
            for sl in range(TOWN // SLAB):
                pt, b_pt = pool.alloc(SLAB)
                pt1 = pt[0:1, :]
                for j in range(16):
                    kb.emit("T", lambda e, j=j, sl=sl, pt1=pt1: e.matmul(
                        pt1, cn["ones"][:, 0:1],
                        y1b[:, j * TOWN + sl * SLAB:j * TOWN + (sl + 1) * SLAB],
                        start=(j == 0), stop=(j == 15)),
                        reads=[cb["ones"], b_y1b[j]], writes=[b_pt])
                kb.emit("V", lambda e, sl=sl, pt1=pt1: e.tensor_scalar(
                    lnr("m", sl), pt1, 1.0 / H, None, ALU.mult),
                    reads=[b_pt], writes=[b_lnrow])
                pt2, b_pt2 = pool.alloc(SLAB)
                pt2a = pt2[0:1, :]
                for j in range(16):
                    kb.emit("S", lambda e, j=j, sl=sl: e.activation(
                        sqsc[j % 2][:],
                        y1b[:, j * TOWN + sl * SLAB:j * TOWN + (sl + 1) * SLAB],
                        AF.Square),
                        reads=[b_y1b[j]], writes=[b_sqsc[j % 2]])
                    kb.emit("T", lambda e, j=j, pt2a=pt2a: e.matmul(
                        pt2a, cn["ones"][:, 0:1], sqsc[j % 2][:],
                        start=(j == 0), stop=(j == 15)),
                        reads=[cb["ones"], b_sqsc[j % 2]], writes=[b_pt2])
                kb.emit("V", lambda e, sl=sl, pt2a=pt2a: e.tensor_scalar(
                    lnr("ms", sl), pt2a, 1.0 / H, None, ALU.mult),
                    reads=[b_pt2], writes=[b_lnrow])
            kb.emit("V", lambda e: e.tensor_tensor(
                lnr("t1"), lnr("m"), lnr("m"), ALU.mult),
                reads=[b_lnrow], writes=[b_lnrow])
            kb.emit("V", lambda e: e.tensor_tensor(
                lnr("r"), lnr("ms"), lnr("t1"), ALU.subtract),
                reads=[b_lnrow], writes=[b_lnrow])
            kb.emit("V", lambda e: e.tensor_scalar(
                lnr("r"), lnr("r"), EPS_LN, None, ALU.add),
                reads=[b_lnrow], writes=[b_lnrow])
            kb.emit("V", lambda e: e.reciprocal(lnr("r"), lnr("r")),
                reads=[b_lnrow], writes=[b_lnrow])
            kb.emit("S", lambda e: e.activation(lnr("r"), lnr("r"), AF.Sqrt),
                reads=[b_lnrow], writes=[b_lnrow])
            kb.emit("V", lambda e: e.tensor_tensor(
                lnr("mr"), lnr("m"), lnr("r"), ALU.mult),
                reads=[b_lnrow], writes=[b_lnrow])
            for sl in range(TOWN // SLAB):
                ptr_, b_ptr = pool.alloc(SLAB)
                ptm_, b_ptm = pool.alloc(SLAB)
                kb.emit("T", lambda e, sl=sl, ptr_=ptr_: e.matmul(
                    ptr_, cn["onesf"][0:1, :], lnr("r", sl),
                    start=True, stop=True),
                    reads=[cb["onesf"], b_lnrow], writes=[b_ptr])
                kb.emit("T", lambda e, sl=sl, ptm_=ptm_: e.matmul(
                    ptm_, cn["onesf"][0:1, :], lnr("mr", sl),
                    start=True, stop=True),
                    reads=[cb["onesf"], b_lnrow], writes=[b_ptm])
                for j in range(16):
                    xs = xin[:, j * TOWN + sl * SLAB:j * TOWN + (sl + 1) * SLAB]
                    kb.emit("V", lambda e, xs=xs, ptr_=ptr_: e.tensor_tensor(
                        xs, xs, ptr_, ALU.mult),
                        reads=[b_xin[j], b_ptr], writes=[b_xin[j]])
                    kb.emit("V", lambda e, xs=xs, ptm_=ptm_: e.tensor_tensor(
                        xs, xs, ptm_, ALU.subtract),
                        reads=[b_xin[j], b_ptm], writes=[b_xin[j]])
                    if make_bf16:
                        kb.emit("V", lambda e, j=j, sl=sl, xs=xs: e.tensor_copy(
                            y1b[:, j * TOWN + sl * SLAB:j * TOWN + (sl + 1) * SLAB], xs),
                            reads=[b_xin[j]], writes=[b_y1b[j]])

        kb.on = True
        if debug:
            kb.emit("G", lambda e: e.dma_start(dbg["d_khat"], khat[:]),
                    reads=b_khat, dma=True)
            kb.emit("G", lambda e: e.dma_start(dbg["d_qx"], qhat[:, :]),
                    reads=b_qhat, dma=True)
            kb.emit("G", lambda e: e.dma_start(dbg["d_ohat"], ohat[:]),
                    reads=b_ohat, dma=True)
            kb.emit("G", lambda e: e.dma_start(dbg["d_zs"], zsilu[:]),
                    reads=b_zsilu, dma=True)
            kb.emit("G", lambda e: e.dma_start(dbg["d_ktm"], ktm[:]),
                    reads=[b_ktm], dma=True)
            kb.emit("G", lambda e: e.dma_start(dbg["d_xcol"], xcol[:]),
                    reads=[b_xcol], dma=True)
            kb.emit("G", lambda e: e.dma_start(dbg["d_bg"], AFA[0:16, 0:5 * TT]),
                    reads=[b_hm], dma=True)
            kb.emit("G", lambda e: e.dma_start(dbg["d_attn"], xin[:]),
                    reads=b_xin, dma=True)

        kb.on = phase >= 7
        layernorm(make_bf16=True)
        kb.on = True

        if debug:
            kb.emit("G", lambda e: e.dma_start(dbg["d_y1"], xin[:]),
                    reads=b_xin, dma=True)

        kb.on = phase >= 8
        kb.barrier()
        Dff = abf(O_DFF, 64 * SLAB)
        b_Dff = [Buf(f"D{nc_}") for nc_ in range(64)]
        w0blk = [abf(O_W0B + i * 2048, 16 * 128) for i in range(2)]
        b_w0blk = [Buf(f"w0blk{i}") for i in range(2)]
        w1buf = [abf(O_W1B + i * SLAB, SLAB) for i in range(4)]
        b_w1buf = [Buf(f"w1buf{i}") for i in range(4)]
        ffnpool = PPool([(psB, 0, 512, b_psB), (poA, 0, 512, b_poA),
                         (poB, 0, 512, b_poB)])

        W0C = 128

        def w0panel(nb, slot):
            src = w0cast[nb]
            kb.emit("Q", lambda e, slot=slot, src=src: e.dma_start(
                w0blk[slot][:], src),
                reads=[b_w0c], writes=[b_w0blk[slot]], dma=True)

        def w1panel(ocg, nc_, slot):
            src = w1cast[nc_, ocg]
            kb.emit("Q", lambda e, slot=slot, src=src: e.dma_start(w1buf[slot][:], src),
                    reads=[b_w1c], writes=[b_w1buf[slot]], dma=True)

        for halfT in range(2):
            t0 = halfT * SLAB
            w0panel(0, 0)
            w0panel(1, 1)
            for nb in range(64):
                slot = nb % 2
                nc_ = nb
                pt, b_pt = ffnpool.alloc(SLAB)
                for f in range(16):
                    kb.emit("T", lambda e, f=f, pt=pt, slot=slot: e.matmul(
                        pt, w0blk[slot][:, f * W0C:(f + 1) * W0C],
                        y1b[:, f * TOWN + t0:f * TOWN + t0 + SLAB],
                        start=(f == 0), stop=(f == 15)),
                        reads=[b_w0blk[slot]] + [b_y1b[f]], writes=[b_pt])
                kb.emit("S", lambda e, nc_=nc_, pt=pt: e.activation(
                    Dff[:, nc_ * SLAB:(nc_ + 1) * SLAB], pt, AF.Gelu,
                    bias=cn["bias0"][:, nc_:nc_ + 1]),
                    reads=[b_pt, cb["bias0"]], writes=[b_Dff[nc_]])
                if nb + 2 < 64:
                    w0panel(nb + 2, slot)
            for ocg in range(4):
                for k in range(4):
                    w1panel(ocg, k, k)
                galloc = [(pp[:, 0:512], b_pp0), (pp[:, 512:1024], b_pp1),
                          (pm[:, 0:512], b_pm), (psA[:, 0:512], b_psA)]
                for nc_ in range(64):
                    slot = nc_ % 4
                    for oc4 in range(4):
                        gp, b_gp = galloc[oc4]
                        kb.emit("T", lambda e, nc_=nc_, oc4=oc4, gp=gp, slot=slot:
                                e.matmul(gp, w1buf[slot][:, oc4 * 128:(oc4 + 1) * 128],
                                         Dff[:, nc_ * SLAB:(nc_ + 1) * SLAB],
                                         start=(nc_ == 0), stop=(nc_ == 63)),
                                reads=[b_w1buf[slot], b_Dff[nc_]], writes=[b_gp])
                    if nc_ + 4 < 64:
                        w1panel(ocg, nc_ + 4, slot)
                for oc4 in range(4):
                    oc = ocg * 4 + oc4
                    gp, b_gp = galloc[oc4]
                    xs = xin[:, oc * TOWN + t0:oc * TOWN + t0 + SLAB]
                    kb.emit("V", lambda e, oc=oc, gp=gp, xs=xs: e.scalar_tensor_tensor(
                        xs, gp, cn["bias1"][:, oc:oc + 1], xs, ALU.add, ALU.add),
                        reads=[b_gp, cb["bias1"], b_xin[oc]], writes=[b_xin[oc]])

        for bb in b_sqsc:
            inherit(bb, b_w0blk)
        layernorm(make_bf16=False)
        for j in range(16):
            kb.emit("Q", lambda e, j=j: e.dma_start(
                yout_d.rearrange("(f p) t -> p f t", p=128)[:, j, :],
                xin[:, j * TOWN:(j + 1) * TOWN]),
                reads=[b_xin[j]], dma=True)
        kb.finish()

    return nc


_cache = {}


def kernel(**inputs):
    in_maps = host_prep(inputs)
    key = "prod"
    if key not in _cache:
        _cache[key] = build_nc(debug=False)
    nc = _cache[key]
    res = run_bass_kernel_spmd(nc, in_maps, core_ids=list(range(8)))
    x = inputs["L0_attn_hidden_states"]
    out = np.empty((B, S, H), dtype=np.float32)
    for c in range(8):
        b, half = c // 2, c % 2
        out[b, half * TOWN:(half + 1) * TOWN, :] = np.asarray(res.results[c]["yout"]).T
    return out


if __name__ == "__main__":
    print("kernel module OK")



# version 09
# speedup vs baseline: 1.2805x; 1.2805x over previous
import numpy as np
import ml_dtypes
import concourse.bass as bass
import concourse.mybir as mybir
from concourse.bass_utils import run_bass_kernel_spmd

F32 = mybir.dt.float32
BF16 = mybir.dt.bfloat16
AF = mybir.ActivationFunctionType
ALU = mybir.AluOpType

B, S, H = 4, 2048, 2048
NK, NV, DK, DV = 8, 16, 64, 64
KDIM, VDIM = NK * DK, NV * DV
NFF = 8192
TOWN, THALO = 1024, 1024
TT = TOWN + THALO
CH = 64
NCH_OWN, NCH_ALL = TOWN // CH, TT // CH
NHP = 8
EPS_L2, EPS_LN = 1e-6, 1e-5
SLAB = 512

PC_K = list(range(0, 4))
PC_V = list(range(4, 12))
PC_BG = [12]
PC_Q = list(range(13, 17))
PC_Z = list(range(17, 25))
NPC = 25
PROJ_COLS = NPC * 128
HALO_PC = PC_K + PC_V + PC_BG
NLEV = 6

DBG = {"mach_n": 999, "mach_parts": 15}

A_BETA, A_B, A_BLOGB, A_EBLOGB, A_EBAR = 0, 1, 2, 3, 4
A_G = 4
XARR = [A_B, A_BLOGB, A_EBLOGB, A_BETA, A_EBAR]
XI_NEGB, XI_BLOGB, XI_EBLOGB, XI_BETA, XI_EBAR = 0, 1, 2, 3, 4


def _host_consts():
    c = {}
    ident = np.eye(128, dtype=np.float32)
    c["identb"] = ident.astype(ml_dtypes.bfloat16)
    c["identf"] = ident.copy()
    sel16 = np.zeros((16, 128), dtype=np.float32)
    for i in range(16):
        sel16[i, :] = ((np.arange(128) // 64) == (i % 2)).astype(np.float32)
    c["sel16"] = sel16.astype(ml_dtypes.bfloat16)
    c["sel16f"] = sel16.copy()
    c["onesf"] = np.ones((1, 128), dtype=np.float32)
    selhp = np.zeros((8, 16, 128), dtype=np.float32)
    for hp in range(8):
        for p in range(128):
            selhp[hp, 2 * hp + p // 64, p] = 1.0
    c["selhp"] = selhp
    bo16 = np.zeros((8, 128, 16), dtype=np.float32)
    for j in range(8):
        for half in range(2):
            bo16[j, half * 64:(half + 1) * 64, 2 * j + half] = 1.0
    c["bo16"] = bo16.astype(ml_dtypes.bfloat16)
    p64 = (np.arange(128) % 64)[:, None]
    t64 = np.arange(64)[None, :]
    c["mus"] = (p64 < t64).astype(np.uint8)
    c["mls"] = (t64 < p64).astype(np.uint8)
    c["mui"] = (p64 <= t64).astype(np.uint8)
    c["is2"] = (p64 == t64).astype(np.float32)
    pd = np.zeros((2, 128, 128), dtype=np.float32)
    for par in range(2):
        for p in range(128):
            pd[par, par * 64 + (p % 64), p] = 1.0
    c["permd2"] = pd.astype(ml_dtypes.bfloat16)
    ph = np.zeros((2, 128, 64), dtype=np.float32)
    for par in range(2):
        for m in range(64):
            ph[par, par * 64 + m, m] = 1.0
    c["permh"] = ph.astype(ml_dtypes.bfloat16)
    c["ones"] = np.ones((128, 128), dtype=np.float32).astype(ml_dtypes.bfloat16)
    return c


def host_prep(inputs):
    x = np.asarray(inputs["L0_attn_hidden_states"], dtype=np.float32)
    Wqkv = np.asarray(inputs["L0_attn_W_qkv"], dtype=np.float32)
    convw = np.asarray(inputs["L0_attn_conv_w"], dtype=np.float32)[:, 0, :]
    Wz = np.asarray(inputs["L0_attn_W_z"], dtype=np.float32)
    Wb = np.asarray(inputs["L0_attn_W_b"], dtype=np.float32)
    Wa = np.asarray(inputs["L0_attn_W_a"], dtype=np.float32)
    Wout = np.asarray(inputs["L0_attn_W_out"], dtype=np.float32)

    q_cols = np.ascontiguousarray(Wqkv[0:KDIM].T)
    k_cols = np.ascontiguousarray(Wqkv[KDIM:2 * KDIM].T)
    v_cols = np.ascontiguousarray(Wqkv[2 * KDIM:].T)
    bg = np.zeros((H, 128), dtype=np.float32)
    bg[:, 0:16] = Wb.T
    bg[:, 16:32] = Wa.T
    wproj = np.concatenate([k_cols, v_cols, bg, q_cols, np.ascontiguousarray(Wz.T)], axis=1)

    convq = convw[0:KDIM]
    convk = convw[KDIM:2 * KDIM]
    convv = convw[2 * KDIM:]
    conv_chunks = [convk[128 * i:128 * (i + 1)] for i in range(4)] + \
                  [convv[128 * i:128 * (i + 1)] for i in range(8)] + \
                  [convq[128 * i:128 * (i + 1)] for i in range(4)]
    convw_sb = np.concatenate(conv_chunks, axis=1)

    consts = _host_consts()
    shared = {
        "wproj": np.ascontiguousarray(
            wproj.reshape(16, 128, PROJ_COLS // 128, 128)
            .transpose(2, 1, 0, 3).reshape(PROJ_COLS, H)),
        "convw": convw_sb,
        "woutT": np.ascontiguousarray(Wout.T),
        "w0T": np.ascontiguousarray(np.asarray(inputs["L0_ffn_W0"], dtype=np.float32).T),
        "w1T": np.ascontiguousarray(np.asarray(inputs["L0_ffn_W1"], dtype=np.float32).T),
        "bias0": np.ascontiguousarray(
            np.asarray(inputs["L0_ffn_bias0"], dtype=np.float32).reshape(64, 128).T),
        "bias1": np.ascontiguousarray(
            np.asarray(inputs["L0_ffn_bias1"], dtype=np.float32).reshape(16, 128).T),
        "dtb": np.asarray(inputs["L0_attn_dt_bias"], dtype=np.float32)[:, None],
        "alog": np.asarray(inputs["L0_attn_A_log"], dtype=np.float32)[:, None],
        "nw2": np.tile(np.asarray(inputs["L0_attn_norm_weight"], dtype=np.float32), 2)[:, None],
        "identb": consts["identb"], "identf": consts["identf"], "sel16": consts["sel16"],
        "sel16f": consts["sel16f"], "onesf": consts["onesf"],
        "selhp": np.ascontiguousarray(consts["selhp"].reshape(8 * 16, 128)),
        "bo16": np.ascontiguousarray(consts["bo16"].reshape(8 * 128, 16)),
        "mus": consts["mus"], "mls": consts["mls"], "mui": consts["mui"],
        "is2": consts["is2"],
        "permd2": np.ascontiguousarray(consts["permd2"].reshape(2 * 128, 128)),
        "permh": np.ascontiguousarray(consts["permh"].reshape(2 * 128, 64)),
        "ones": consts["ones"],
    }
    zeros_halo = np.zeros((H, THALO), dtype=np.float32)
    in_maps = []
    for c in range(8):
        b, half = c // 2, c % 2
        xT = np.ascontiguousarray(x[b].T)
        m = dict(shared)
        m["xin"] = np.ascontiguousarray(xT[:, half * TOWN:(half + 1) * TOWN])
        m["xhalo"] = np.ascontiguousarray(xT[:, 0:THALO]) if half == 1 else zeros_halo
        in_maps.append(m)
    return in_maps


class Buf:
    __slots__ = ("writer", "readers", "name")

    def __init__(self, name=""):
        self.writer = None
        self.readers = {}
        self.name = name


class KB:
    ENGS = ("T", "V", "S", "G", "Q")

    NDMA_G, NDMA_Q = 14, 6

    def __init__(self, nc, es):
        self.nc = nc
        self.eng = {"T": nc.tensor, "V": nc.vector, "S": nc.scalar,
                    "G": nc.gpsimd, "Q": nc.sync}
        self.sems = {n: es.enter_context(nc.semaphore("sem_" + n))
                     for n in ("T", "V", "S", "G")}
        for i in range(self.NDMA_G):
            self.sems[f"DG{i}"] = es.enter_context(nc.semaphore(f"sem_DG{i}"))
        for i in range(self.NDMA_Q):
            self.sems[f"DQ{i}"] = es.enter_context(nc.semaphore(f"sem_DQ{i}"))
        self.cnt = {k: 0 for k in self.sems}
        self.dma_i = {"G": 0, "Q": 0}
        self.vc = {e: {} for e in self.ENGS}
        self.on = True
        self.drained = {e: 0 for e in self.ENGS}

    def _wait(self, eng, sem, val):
        if val <= 0 or sem == eng:
            return
        if self.vc[eng].get(sem, 0) >= val:
            return
        self.eng[eng].wait_ge(self.sems[sem], val)
        self.vc[eng][sem] = val

    def emit(self, eng, fn, reads=(), writes=(), dma=False):
        if not self.on:
            return None
        if eng in ("V", "S", "G"):
            for b in reads:
                if (b.writer and b.writer[0] == eng
                        and self.cnt[eng] - b.writer[1] <= 1
                        and self.drained[eng] < b.writer[1]):
                    self.eng[eng].drain()
                    self.drained[eng] = self.cnt[eng]
                    break
        need = {}
        for b in reads:
            if b.writer and b.writer[1] > need.get(b.writer[0], 0):
                need[b.writer[0]] = b.writer[1]
        for b in writes:
            if b.writer and b.writer[1] > need.get(b.writer[0], 0):
                need[b.writer[0]] = b.writer[1]
            for sem, val in b.readers.items():
                if val > need.get(sem, 0):
                    need[sem] = val
        if dma:
            q = "Q" if eng == "Q" else "G"
            n = self.NDMA_Q if q == "Q" else self.NDMA_G
            dsem = f"D{q}{self.dma_i[q] % n}"
            self.dma_i[q] += 1
            if self.cnt[dsem] > need.get(dsem, 0):
                need[dsem] = self.cnt[dsem]
        for sem in sorted(need):
            self._wait(eng, sem, need[sem])
        instr = fn(self.eng[eng])
        if dma:
            self.cnt[dsem] += 16
            instr.then_inc(self.sems[dsem], 16)
            tok = (dsem, self.cnt[dsem])
        else:
            self.cnt[eng] += 1
            instr.then_inc(self.sems[eng], 1)
            tok = (eng, self.cnt[eng])
        for b in reads:
            if b.readers.get(tok[0], 0) < tok[1]:
                b.readers[tok[0]] = tok[1]
        for b in writes:
            b.writer = tok
            b.readers = {}
        return tok

    def barrier(self):
        snap = dict(self.cnt)
        for e in self.ENGS:
            for sem, val in snap.items():
                if sem != e and val > 0:
                    self._wait(e, sem, val)

    def finish(self):
        snap = dict(self.cnt)
        for sem, val in snap.items():
            if sem != "Q" and val > 0:
                self._wait("Q", sem, val)


class PPool:

    def __init__(self, banks):
        self.banks = banks
        self.cur = 0
        self.off = [0] * len(banks)

    def alloc(self, w):
        for _ in range(2 * len(self.banks) + 1):
            t, c0, ncols, buf = self.banks[self.cur]
            if self.off[self.cur] + w <= ncols:
                off = self.off[self.cur]
                self.off[self.cur] += w
                ap = t[:, c0 + off:c0 + off + w]
                self.cur = (self.cur + 1) % len(self.banks)
                return ap, buf
            self.off[self.cur] = 0
        raise RuntimeError("psum pool alloc")


def build_nc(debug=False, phase=8):
    from contextlib import ExitStack
    nc = bass.Bass("TRN2", target_bir_lowering=False)
    es = ExitStack()
    dr = {}

    def din(name, shape, dt=F32):
        dr[name] = nc.dram_tensor(name, list(shape), dt, kind="ExternalInput").ap()

    din("xin", (H, TOWN)); din("xhalo", (H, THALO))
    din("wproj", (PROJ_COLS, H)); din("convw", (128, 64))
    din("woutT", (VDIM, H)); din("w0T", (H, NFF)); din("w1T", (NFF, H))
    din("bias0", (128, 64)); din("bias1", (128, 16))
    din("dtb", (16, 1)); din("alog", (16, 1)); din("nw2", (128, 1))
    din("identb", (128, 128), BF16); din("identf", (128, 128))
    din("sel16", (16, 128), BF16); din("bo16", (8 * 128, 16), BF16)
    din("sel16f", (16, 128)); din("onesf", (1, 128))
    din("selhp", (8 * 16, 128))
    din("mus", (128, 64), mybir.dt.uint8); din("mls", (128, 64), mybir.dt.uint8)
    din("mui", (128, 64), mybir.dt.uint8); din("is2", (128, 64))
    din("permd2", (2 * 128, 128), BF16); din("permh", (2 * 128, 64), BF16)
    din("ones", (128, 128), BF16)
    yout_d = nc.dram_tensor("yout", [H, TOWN], F32, kind="ExternalOutput").ap()

    dbg_specs = [("d_raws", 128, 2 * 520), ("d_tails", 128, 64),
                 ("d_khat", 128, 4 * TT), ("d_qx", 128, 4 * TOWN),
                 ("d_bg", 16, 5 * TT), ("d_ohat", 128, 8 * TOWN),
                 ("d_zs", 128, 8 * TOWN), ("d_attn", 128, 16 * TOWN),
                 ("d_y1", 128, 16 * TOWN), ("d_ktm", 128, 16 * 512),
                 ("d_xcol", 128, NCH_ALL * 40)]
    dbg = {}
    if debug:
        for name, p, f in dbg_specs:
            dbg[name] = nc.dram_tensor(name, [p, f], F32, kind="ExternalOutput").ap()

    with es:
        kb = KB(nc, es)

        def sb(name, p, f, dt=F32):
            return es.enter_context(nc.sbuf_tensor(name, [p, f], dt))

        cn = {}
        for name, p, f, dt in [
                ("convw", 128, 64, F32), ("bias0", 128, 64, F32), ("bias1", 128, 16, F32),
                ("dtb", 16, 1, F32), ("alog", 16, 1, F32), ("nw2", 128, 1, F32),
                ("identb", 128, 128, BF16), ("identf", 128, 128, F32),
                ("sel16", 16, 128, BF16), ("sel16f", 16, 128, F32),
                ("onesf", 1, 128, F32),
                ("mus", 128, 64, mybir.dt.uint8), ("mls", 128, 64, mybir.dt.uint8),
                ("mui", 128, 64, mybir.dt.uint8), ("is2", 128, 64, F32),
                ("ones", 128, 128, BF16)]:
            cn[name] = sb("c_" + name, p, f, dt)
        bo16 = sb("c_bo16", 128, 8 * 16, BF16)
        selhp = sb("c_selhp", 16, 8 * 128, F32)
        permd2 = sb("c_permd2", 128, 2 * 128, BF16)
        permh = sb("c_permh", 128, 2 * 64, BF16)
        cb = {k: Buf("c_" + k) for k in list(cn) + ["bo16", "permd2", "permh", "selhp"]}
        for name in cn:
            kb.emit("Q", lambda e, n=name: e.dma_start(cn[n][:], dr[n]),
                    writes=[cb[name]], dma=True)
        kb.emit("Q", lambda e: e.dma_start(
            bo16[:].rearrange("p (j m) -> p j m", m=16),
            dr["bo16"].rearrange("(j p) m -> p j m", p=128)),
            writes=[cb["bo16"]], dma=True)
        kb.emit("Q", lambda e: e.dma_start(
            selhp[:].rearrange("k (j m) -> k j m", m=128),
            dr["selhp"].rearrange("(j k) m -> k j m", k=16)),
            writes=[cb["selhp"]], dma=True)
        kb.emit("Q", lambda e: e.dma_start(
            permd2[:].rearrange("p (j m) -> p j m", m=128),
            dr["permd2"].rearrange("(j p) m -> p j m", p=128)),
            writes=[cb["permd2"]], dma=True)
        kb.emit("Q", lambda e: e.dma_start(
            permh[:].rearrange("p (j m) -> p j m", m=64),
            dr["permh"].rearrange("(j p) m -> p j m", p=128)),
            writes=[cb["permh"]], dma=True)

        pp = es.enter_context(nc.psum_tensor("pp", [128, 1024], F32))
        pm = es.enter_context(nc.psum_tensor("pm", [128, 512], F32))
        pmb = es.enter_context(nc.psum_tensor("pmb", [128, 1024], BF16))
        psA = es.enter_context(nc.psum_tensor("psA", [128, 512], F32))
        psB = es.enter_context(nc.psum_tensor("psB", [128, 512], F32))
        poA = es.enter_context(nc.psum_tensor("poA", [128, 512], F32))
        poB = es.enter_context(nc.psum_tensor("poB", [128, 512], F32))
        b_pp0, b_pp1 = Buf("pp0"), Buf("pp1")
        b_pm, b_pmb = Buf("pm"), Buf("pmb")
        b_psA, b_psB, b_poA, b_poB = Buf("psA"), Buf("psB"), Buf("poA"), Buf("poB")
        pmbf = pmb[:].bitcast(F32)
        pool = PPool([(pp, 0, 512, b_pp0), (pp, 512, 512, b_pp1),
                      (pmbf, 0, 512, b_pmb)])
        poolS = PPool([(pm, 0, 512, b_pm)])
        poolb = PPool([(pmb, 0, 1024, b_pmb)])

        ABF = sb("ABF", 128, 55296, BF16)
        AFA = sb("AFA", 128, 19840, F32)

        def abf(off, n, p0=0, p1=128):
            return ABF[p0:p1, off:off + n]

        def af(off, n, p0=0, p1=128):
            return AFA[p0:p1, off:off + n]

        def inherit(newbufs, oldbufs):
            if isinstance(newbufs, Buf):
                newbufs = [newbufs]
            for nb in newbufs:
                for ob in (oldbufs if not isinstance(oldbufs, Buf) else [oldbufs]):
                    if ob.writer and ob.writer[1] > nb.readers.get(ob.writer[0], 0):
                        nb.readers[ob.writer[0]] = ob.writer[1]
                    for sem, val in ob.readers.items():
                        if val > nb.readers.get(sem, 0):
                            nb.readers[sem] = val

        O_XS = 0
        O_Q = 16384
        O_KHAT = 20480
        O_VHAT = 28672
        O_OHAT = 28672
        O_MACH = 36864
        O_ZS = 36864
        O_WBUF, O_RAWS, O_DTILE = 45056, 49152, 50192
        O_SQSC = 51216
        O_KTM = 45056
        O_VTM = 0
        O_OMOV = 0
        O_WOBUF = 20480
        O_Y1B, O_DFF, O_W0B, O_W1B = 0, 16384, 49152, 53248
        O_HM = 0
        O_TMP16, O_BGRAW, O_SSK = 10240, 12288, 14336
        O_SSQ = 8192
        O_XINT = 10240
        O_HMP = 10240
        O_WEX, O_SST = 0, 2048
        O_XIN = 0
        O_XCOL, O_SSO = 16384, 16384
        O_EBCHP = 17664
        O_LNM, O_LNR, O_LNT = 16384, 17664, 18688
        O_MISC = 19712

        xslab = [abf(O_XS + i * 8192, 8192) for i in range(2)]
        b_xslab = [Buf(f"xslab{i}") for i in range(2)]

        khat = abf(O_KHAT, 4 * TT)
        vhat = abf(O_VHAT, 8 * TT)
        qhat = abf(O_Q, 4 * TOWN)
        zsilu = abf(O_ZS, 8 * TOWN)
        bgraw = af(O_BGRAW, TT, 0, 32)
        b_khat = [Buf(f"khat{j}") for j in range(4)]
        b_vhat = [Buf(f"vhat{j}") for j in range(8)]
        b_qhat = [Buf(f"q{j}") for j in range(4)]
        b_zsilu = [Buf(f"zs{j}") for j in range(8)]
        b_bgraw = Buf("bgraw")

        wbuf = [abf(O_WBUF + i * 2048, 2048) for i in range(2)]
        b_wbuf = [Buf(f"wbuf{i}") for i in range(2)]
        raws = [abf(O_RAWS + i * 520, SLAB + 4) for i in range(2)]
        b_raws = [Buf(f"raws{i}") for i in range(2)]
        tails = sb("tails", 128, 16 * 4, F32)
        b_tails = Buf("tails")
        tail3 = sb("tail3", 128, 16 * 4, BF16)
        b_tail3 = Buf("tail3")
        dtile = [abf(O_DTILE + i * 512, 4 * 128) for i in range(2)]
        b_dtile = [Buf(f"dt{i}") for i in range(2)]

        def wpanel_load(oc, slot):
            if oc in PC_Z:
                src = wpcast[oc - PC_Z[0]]
                kb.emit("Q", lambda e, slot=slot, src=src: e.dma_start(
                    wbuf[slot][:], src),
                    reads=[b_wpc], writes=[b_wbuf[slot]], dma=True)
                return
            src = dr["wproj"][oc * 128:(oc + 1) * 128, :]
            kb.emit("G", lambda e, slot=slot, src=src: e.dma_start(
                wbuf[slot][:], src),
                writes=[b_wbuf[slot]], dma=True)

        def store_for(oc):
            if oc in PC_K:
                j = oc - PC_K[0]
                return khat, j, b_khat[j], TT, True, "kv"
            if oc in PC_V:
                j = oc - PC_V[0]
                return vhat, j, b_vhat[j], TT, True, "kv"
            if oc in PC_Q:
                j = oc - PC_Q[0]
                return qhat, j, b_qhat[j], TOWN, True, "qx"
            if oc in PC_Z:
                j = oc - PC_Z[0]
                return zsilu, j, b_zsilu[j], TOWN, False, "z"
            return None, 0, b_bgraw, TT, False, "bg"

        def conv_slot(oc):
            return oc if oc < 12 else 12 + (oc - PC_Q[0])

        kb.emit("V", lambda e: e.memset(tails[:], 0.0), writes=[b_tails])

        def proj_pass(chunks, src_dram, tok0, ntok, qx_tails=False):
            nslab = ntok // SLAB
            for sl in range(nslab):
                xs_i = sl % 2
                xs, b_xs = xslab[xs_i], b_xslab[xs_i]
                kb.emit("G", lambda e, xs=xs, sl=sl, src_dram=src_dram: e.dma_start(
                    xs[:].rearrange("p (f t) -> p f t", t=SLAB),
                    src_dram.rearrange("(f p) t -> p f t", p=128)[:, :, sl * SLAB:(sl + 1) * SLAB]),
                    reads=[], writes=[b_xs], dma=True)
                for idx, oc in enumerate(chunks):
                    slot = idx % 2
                    if idx < 2:
                        wpanel_load(oc, slot)
                    dest, j, b_dest, ncols, do_conv, kind = store_for(oc)
                    ds = idx % 2
                    cs = conv_slot(oc) if do_conv else -1
                    if do_conv:
                        for tap in range(4):
                            kb.emit("V", lambda e, ds=ds, tap=tap, cs=cs: e.tensor_scalar(
                                dtile[ds][:, tap * 128:(tap + 1) * 128], cn["identb"][:],
                                cn["convw"][:, cs * 4 + tap:cs * 4 + tap + 1], None,
                                ALU.mult),
                                reads=[cb["identb"], cb["convw"], b_dtile[ds]],
                                writes=[b_dtile[ds]])
                        if qx_tails and sl == 0 and kind == "qx":
                            tp, b_tp = pool.alloc(4)
                            for f in range(16):
                                kb.emit("T", lambda e, f=f, tp=tp, slot=slot: e.matmul(
                                    tp, wbuf[slot][:, f * 128:(f + 1) * 128],
                                    tail3[:, f * 4:(f + 1) * 4],
                                    start=(f == 0), stop=(f == 15)),
                                    reads=[b_wbuf[slot], b_tail3], writes=[b_tp])
                            kb.emit("V", lambda e, cs=cs, tp=tp: e.tensor_copy(
                                tails[:, cs * 4:(cs + 1) * 4], tp),
                                reads=[b_tp], writes=[b_tails])
                    pt, b_pt = (poolS if (not do_conv and kind == "z") else pool).alloc(SLAB)
                    for f in range(16):
                        kb.emit("T", lambda e, f=f, pt=pt, slot=slot, xs=xs: e.matmul(
                            pt, wbuf[slot][:, f * 128:(f + 1) * 128],
                            xs[:, f * SLAB:(f + 1) * SLAB],
                            start=(f == 0), stop=(f == 15)),
                            reads=[b_wbuf[slot], b_xs], writes=[b_pt])
                    gtok = tok0 + sl * SLAB
                    dtok = gtok if kind in ("kv", "bg") else gtok - THALO
                    if kind == "bg":
                        kb.emit("V", lambda e, pt=pt, gtok=gtok: e.tensor_copy(
                            bgraw[:, gtok:gtok + SLAB], pt[0:32, :]),
                            reads=[b_pt], writes=[b_bgraw])
                    elif do_conv:
                        rs = idx % 2
                        stage, b_stage = raws[rs], b_raws[rs]
                        kb.emit("V", lambda e, stage=stage, cs=cs: e.tensor_copy(
                            stage[:, 0:4], tails[:, cs * 4:(cs + 1) * 4]),
                            reads=[b_tails], writes=[b_stage])
                        kb.emit("V", lambda e, stage=stage, pt=pt: e.tensor_copy(
                            stage[:, 4:4 + SLAB], pt), reads=[b_pt], writes=[b_stage])
                        kb.emit("V", lambda e, stage=stage, cs=cs: e.tensor_copy(
                            tails[:, cs * 4:(cs + 1) * 4], stage[:, SLAB:SLAB + 4]),
                            reads=[b_stage], writes=[b_tails])
                        pc, b_pc = poolS.alloc(SLAB)
                        for tap in range(4):
                            kb.emit("T", lambda e, ds=ds, tap=tap, pc=pc, stage=stage:
                                    e.matmul(pc, dtile[ds][:, tap * 128:(tap + 1) * 128],
                                             stage[:, 1 + tap:1 + tap + SLAB],
                                             start=(tap == 0), stop=(tap == 3)),
                                    reads=[b_dtile[ds], b_stage], writes=[b_pc])
                        kb.emit("S", lambda e, dest=dest, j=j, dtok=dtok, ncols=ncols,
                                pc=pc: e.activation(
                                    dest[:, j * ncols + dtok:j * ncols + dtok + SLAB],
                                    pc, AF.Silu),
                                reads=[b_pc], writes=[b_dest])
                    else:
                        kb.emit("S", lambda e, dest=dest, j=j, dtok=dtok, ncols=ncols,
                                pt=pt: e.activation(
                                    dest[:, j * ncols + dtok:j * ncols + dtok + SLAB],
                                    pt, AF.Silu),
                                reads=[b_pt], writes=[b_dest])
                    if idx + 2 < len(chunks):
                        wpanel_load(chunks[idx + 2], (idx + 2) % 2)

        proj_pass(HALO_PC, dr["xhalo"], 0, THALO)
        kb.emit("G", lambda e: e.dma_start(
            tail3[:].rearrange("p (f u) -> p f u", u=4),
            dr["xhalo"].rearrange("(f p) t -> p f t", p=128)[:, :, THALO - 4:THALO]),
            writes=[b_tail3], dma=True)
        proj_pass(PC_K + PC_V + PC_BG + PC_Q, dr["xin"], THALO, TOWN, qx_tails=True)

        kb.on = phase >= 2
        ssk = af(O_SSK, TT, 0, 16)
        ssq = af(O_SSQ, TOWN, 0, 16)
        b_ssk, b_ssq = Buf("ssk"), Buf("ssq")
        sqsc = [abf(O_SQSC + i * SLAB, SLAB) for i in range(2)]
        b_sqsc = [Buf(f"sqsc{i}") for i in range(2)]

        def sumsq(dest, b_dest, src, b_srcs, nj, ncols):
            for sl in range(ncols // SLAB):
                pt, b_pt = pool.alloc(SLAB)
                pt16 = pt[0:16, :]
                for j in range(nj):
                    kb.emit("S", lambda e, j=j, sl=sl: e.activation(
                        sqsc[j % 2][:],
                        src[:, j * ncols + sl * SLAB:j * ncols + (sl + 1) * SLAB],
                        AF.Square),
                        reads=[b_srcs[j]], writes=[b_sqsc[j % 2]])
                    kb.emit("T", lambda e, j=j, pt16=pt16: e.matmul(
                        pt16, bo16[:, j * 16:(j + 1) * 16], sqsc[j % 2][:],
                        start=(j == 0), stop=(j == nj - 1)),
                        reads=[cb["bo16"], b_sqsc[j % 2]], writes=[b_pt])
                kb.emit("V", lambda e, sl=sl, pt16=pt16: e.tensor_copy(
                    dest[:, sl * SLAB:(sl + 1) * SLAB], pt16),
                    reads=[b_pt], writes=[b_dest])

        sumsq(ssk, b_ssk, khat, b_khat, 4, TT)
        sumsq(ssq, b_ssq, qhat, b_qhat, 4, TOWN)
        for tn, tb, ncols, scale in [(ssk, b_ssk, TT, 1.0), (ssq, b_ssq, TOWN, 1.0 / 64.0)]:
            kb.emit("V", lambda e, tn=tn, ncols=ncols: e.tensor_scalar(
                tn[:, 0:ncols], tn[:, 0:ncols], EPS_L2, None, ALU.add),
                reads=[tb], writes=[tb])
            kb.emit("V", lambda e, tn=tn, ncols=ncols: e.reciprocal(
                tn[:, 0:ncols], tn[:, 0:ncols]), reads=[tb], writes=[tb])
            kb.emit("S", lambda e, tn=tn, ncols=ncols, scale=scale: e.activation(
                tn[:, 0:ncols], tn[:, 0:ncols], AF.Sqrt, scale=scale),
                reads=[tb], writes=[tb])

        def l2apply(dest, b_dests, rn, b_rn, nj, ncols):
            for j in range(nj):
                for sl in range(ncols // SLAB):
                    pt, b_pt = pool.alloc(SLAB)
                    kb.emit("T", lambda e, j=j, sl=sl, pt=pt: e.matmul(
                        pt, selhp[:, j * 128:(j + 1) * 128],
                        rn[:, sl * SLAB:(sl + 1) * SLAB],
                        start=True, stop=True),
                        reads=[cb["selhp"], b_rn], writes=[b_pt])
                    sl_ap = dest[:, j * ncols + sl * SLAB:j * ncols + (sl + 1) * SLAB]
                    kb.emit("V", lambda e, sl_ap=sl_ap, pt=pt: e.tensor_tensor(
                        sl_ap, sl_ap, pt, ALU.mult),
                        reads=[b_dests[j], b_pt], writes=[b_dests[j]])

        l2apply(khat, b_khat, ssk, b_ssk, 4, TT)
        l2apply(qhat, b_qhat, ssq, b_ssq, 4, TOWN)

        kb.on = phase >= 3
        b_hm = Buf("hm")
        ebc = af(O_MISC + 4, NCH_ALL, 0, 16)
        b_ebc = Buf("ebc")
        tmp16 = af(O_TMP16, TT, 0, 16)
        b_tmp16 = Buf("tmp16")
        negea = af(O_MISC, 1, 0, 16)
        b_negea = Buf("negea")
        inherit(b_hm, [b_ssq])

        def hma(a):
            return AFA[0:16, O_HM + a * TT:O_HM + (a + 1) * TT]

        kb.emit("S", lambda e: e.activation(negea[:], cn["alog"][:], AF.Exp),
                reads=[cb["alog"]], writes=[b_negea])
        kb.emit("V", lambda e: e.tensor_scalar(negea[:], negea[:], -1.0, None, ALU.mult),
                reads=[b_negea], writes=[b_negea])
        gpre16 = af(O_HM + A_G * TT, TT, 0, 16)
        b_gpre = Buf("gpre16")
        inherit(b_gpre, [b_ssq])
        kb.emit("G", lambda e: e.dma_start(gpre16, bgraw[16:32, :]),
                reads=[b_bgraw], writes=[b_gpre], dma=True)
        kb.emit("S", lambda e: e.activation(tmp16[:], gpre16, AF.Exp,
                                            bias=cn["dtb"][:]),
                reads=[b_gpre, cb["dtb"]], writes=[b_tmp16])
        kb.emit("S", lambda e: e.add(tmp16[:], tmp16[:], 1.0),
                reads=[b_tmp16], writes=[b_tmp16])
        kb.emit("S", lambda e: e.activation(tmp16[:], tmp16[:], AF.Ln),
                reads=[b_tmp16], writes=[b_tmp16])
        inherit(b_hm, [b_gpre])
        kb.emit("V", lambda e: e.tensor_scalar(hma(A_G), tmp16[:], negea[:], None, ALU.mult),
                reads=[b_tmp16, b_negea], writes=[b_hm])
        for c in range(NCH_ALL):
            kb.emit("V", lambda e, c=c: e.tensor_tensor_scan(
                hma(A_B)[:, c * CH:(c + 1) * CH], hma(A_G)[:, c * CH:(c + 1) * CH],
                hma(A_G)[:, c * CH:(c + 1) * CH], 0.0, ALU.add, ALU.bypass),
                reads=[b_hm], writes=[b_hm])
        kb.emit("S", lambda e: e.activation(tmp16[:], bgraw[0:16, :], AF.Exp, scale=-1.0),
                reads=[b_bgraw], writes=[b_tmp16])
        kb.emit("S", lambda e: e.add(tmp16[:], tmp16[:], 1.0),
                reads=[b_tmp16], writes=[b_tmp16])
        kb.emit("S", lambda e: e.activation(tmp16[:], tmp16[:], AF.Ln),
                reads=[b_tmp16], writes=[b_tmp16])
        kb.emit("V", lambda e: e.tensor_tensor(hma(A_BLOGB), hma(A_B), tmp16[:],
                                               ALU.subtract),
                reads=[b_hm, b_tmp16], writes=[b_hm])
        kb.emit("S", lambda e: e.activation(hma(A_EBLOGB), hma(A_BLOGB), AF.Exp),
                reads=[b_hm], writes=[b_hm])
        kb.emit("S", lambda e: e.activation(hma(A_BETA), bgraw[0:16, :], AF.Sigmoid),
                reads=[b_bgraw], writes=[b_hm])
        for c in range(NCH_ALL):
            kb.emit("V", lambda e, c=c: e.tensor_scalar(
                hma(A_EBAR)[:, c * CH:(c + 1) * CH], hma(A_B)[:, c * CH:(c + 1) * CH],
                hma(A_B)[:, c * CH + CH - 1:c * CH + CH], None, ALU.subtract),
                reads=[b_hm], writes=[b_hm])
            kb.emit("V", lambda e, c=c: e.tensor_copy(
                ebc[:, c:c + 1], hma(A_B)[:, c * CH + CH - 1:c * CH + CH]),
                reads=[b_hm], writes=[b_ebc])
        kb.emit("S", lambda e: e.activation(hma(A_EBAR), hma(A_EBAR), AF.Exp, scale=-1.0),
                reads=[b_hm], writes=[b_hm])
        kb.emit("S", lambda e: e.activation(ebc[:], ebc[:], AF.Exp),
                reads=[b_ebc], writes=[b_ebc])

        hmP = af(O_HMP, 2 * TT, 0, 16)
        b_hmP = Buf("hmP")

        kb.on = phase >= 4
        ktm = abf(O_KTM, 16 * 512)
        vtm = abf(O_VTM, 16 * 1024)
        b_ktm, b_vtm = Buf("ktm"), Buf("vtm")
        inherit(b_ktm, b_wbuf + b_raws + b_dtile + b_sqsc)
        inherit(b_vtm, b_xslab)

        def transpose_bf(src_ap, b_srcs, dst_ap, b_dst):
            tp, b_tp = poolb.alloc(128)
            kb.emit("T", lambda e, tp=tp, src_ap=src_ap: e.transpose(
                tp, src_ap, cn["identb"][:]),
                reads=list(b_srcs) + [cb["identb"]], writes=[b_tp])
            kb.emit("V", lambda e, tp=tp, dst_ap=dst_ap: e.tensor_copy(dst_ap, tp),
                    reads=[b_tp], writes=[b_dst])

        for blk in range(16):
            for j in range(4):
                transpose_bf(khat[:, j * TT + blk * 128:j * TT + (blk + 1) * 128],
                             [b_khat[j]],
                             ktm[:, blk * 512 + j * 128:blk * 512 + (j + 1) * 128], b_ktm)
            for j in range(8):
                transpose_bf(vhat[:, j * TT + blk * 128:j * TT + (blk + 1) * 128],
                             [b_vhat[j]],
                             vtm[:, blk * 1024 + j * 128:blk * 1024 + (j + 1) * 128], b_vtm)

        xint = af(O_XINT, 2 * TT, 0, 40)
        b_xint = Buf("xint")
        inherit(b_xint, [b_tmp16])
        inherit(b_xint, [b_bgraw])
        xcol = af(O_XCOL, NCH_ALL * 40)
        b_xcol = Buf("xcol")
        for ai, arr in enumerate(XARR):
            for par in range(2):
                kb.emit("G", lambda e, ai=ai, arr=arr, par=par: e.dma_start(
                    xint[ai * 8:ai * 8 + 8, :].rearrange("p (c u) -> p c u", u=128)[:, :, par * 64:par * 64 + 64],
                    hma(arr)[par:16:2, :].rearrange("p (c u) -> p c u", u=64)),
                    reads=[b_hm], writes=[b_xint], dma=True)
        for c in range(NCH_ALL):
            tp, b_tp = pool.alloc(40)
            kb.emit("T", lambda e, c=c, tp=tp: e.transpose(
                tp, xint[:, c * 128:(c + 1) * 128], cn["identf"][0:40, 0:40]),
                reads=[b_xint, cb["identf"]], writes=[b_tp])
            kb.emit("V", lambda e, c=c, tp=tp: e.tensor_copy(
                xcol[:, c * 40:(c + 1) * 40], tp), reads=[b_tp], writes=[b_xcol])
            kb.emit("V", lambda e, c=c: e.tensor_scalar(
                xcol[:, c * 40:c * 40 + 8], xcol[:, c * 40:c * 40 + 8], -1.0, None,
                ALU.mult), reads=[b_xcol], writes=[b_xcol])

        def XC(xi, c, hp):
            return xcol[:, c * 40 + xi * 8 + hp:c * 40 + xi * 8 + hp + 1]

        inherit(b_hmP, [b_xint])
        for ai, arr in enumerate([A_BLOGB, A_B]):
            kb.emit("V", lambda e, ai=ai, arr=arr: e.tensor_copy(
                hmP[:, :].rearrange("p (c u) -> p c u", u=128)[:, :, ai * 64:ai * 64 + 64],
                hma(arr).rearrange("p (c u) -> p c u", u=64)),
                reads=[b_hm], writes=[b_hmP])

        ebchp = af(O_EBCHP, 8 * 32)
        b_ebchp = Buf("ebchp")
        for hp in range(8):
            tp, b_tp = pool.alloc(32)
            kb.emit("T", lambda e, hp=hp, tp=tp: e.matmul(
                tp, selhp[:, hp * 128:(hp + 1) * 128], ebc[:, :],
                start=True, stop=True),
                reads=[cb["selhp"], b_ebc], writes=[b_tp])
            kb.emit("V", lambda e, hp=hp, tp=tp: e.tensor_copy(
                ebchp[:, hp * 32:(hp + 1) * 32], tp), reads=[b_tp], writes=[b_ebchp])

        def EBCC(c, hp):
            return ebchp[:, hp * 32 + c:hp * 32 + c + 1]

        MACH_SIZES = [("wNT", 128), ("wNP", 128), ("wR", 256), ("wGL", 64),
                      ("wDT", 64), ("wKB", 64), ("wPT", 64), ("wQE", 64),
                      ("exmNT", 64), ("exmNP", 64), ("exmFTI", 64)]
        mt, moff = {}, O_MACH
        for nm, width in MACH_SIZES:
            mt[nm] = [abf(moff + h * 960, width) for h in range(8)]
        off2 = 0
        for nm, width in MACH_SIZES:
            for h in range(8):
                mt[nm][h] = abf(O_MACH + h * 960 + off2, width)
            off2 += width
        wNT, wNP, wR = mt["wNT"], mt["wNP"], mt["wR"]
        wGL, wDT = mt["wGL"], mt["wDT"]
        wKB, wPT, wQE = mt["wKB"], mt["wPT"], mt["wQE"]
        exmNT, exmNP, exmFTI = mt["exmNT"], mt["exmNP"], mt["exmFTI"]
        b_wNT = [Buf(f"wNT{h}") for h in range(8)]
        b_wNP = [Buf(f"wNP{h}") for h in range(8)]
        b_wR = [Buf(f"wR{h}") for h in range(8)]
        b_wGL = [Buf(f"wGL{h}") for h in range(8)]
        b_wDT = [Buf(f"wDT{h}") for h in range(8)]
        b_wKB = [Buf(f"wKB{h}") for h in range(8)]
        b_wPT = [Buf(f"wPT{h}") for h in range(8)]
        b_wQE = [Buf(f"wQE{h}") for h in range(8)]
        b_exmNT = [Buf(f"exmNT{h}") for h in range(8)]
        b_exmNP = [Buf(f"exmNP{h}") for h in range(8)]
        b_exmFTI = [Buf(f"exmFTI{h}") for h in range(8)]
        wEx = [af(O_WEX + h * 256, 256) for h in range(8)]
        b_wEx = [Buf(f"wEx{h}") for h in range(8)]
        wE1 = [w[:, 192:256] for w in wEx]
        b_wE1 = b_wEx
        Sst = [af(O_SST + h * 128, 128) for h in range(8)]
        b_Sst = [Buf(f"Sst{h}") for h in range(8)]
        Sb16t = sb("Sb16", 128, 8 * 64, BF16)
        Sb16 = [Sb16t[:, h * 64:(h + 1) * 64] for h in range(8)]
        b_Sb16 = [Buf(f"Sb16{h}") for h in range(8)]
        ohat = abf(O_OHAT, 8 * TOWN)
        b_ohat = [Buf(f"ohat{h}") for h in range(8)]
        mach_bufs = (b_wNT + b_wNP + b_wR + b_wGL + b_wDT
                     + b_wKB + b_wPT + b_wQE + b_exmNT + b_exmNP + b_exmFTI)
        for bb in mach_bufs + b_ohat:
            inherit(bb, [b_vtm if False else b_vhat[0]])
            for bv in b_vhat:
                inherit(bb, bv)
        for bb in b_wEx + b_Sst:
            inherit(bb, [b_hm])
        for h in range(8):
            kb.emit("V", lambda e, h=h: e.memset(Sst[h][:, 0:64], 0.0),
                    writes=[b_Sst[h]])
            kb.emit("V", lambda e, h=h: e.memset(Sb16[h], 0.0), writes=[b_Sb16[h]])
            kb.emit("V", lambda e, h=h: e.memset(exmNT[h][:], 0.0), writes=[b_exmNT[h]])
            kb.emit("V", lambda e, h=h: e.memset(exmNP[h][:], 0.0), writes=[b_exmNP[h]])
            kb.emit("V", lambda e, h=h: e.memset(exmFTI[h][:], 0.0),
                    writes=[b_exmFTI[h]])

        poolMV = PPool([(pp, 0, 512, b_pp0), (pp, 512, 512, b_pp1)])
        poolMS = PPool([(pm, 0, 512, b_pm), (pmbf, 0, 512, b_pmb)])

        def mach_chunk(c):
            parts = DBG["mach_parts"]
            is_own = c >= NCH_ALL - NCH_OWN
            co = c - (NCH_ALL - NCH_OWN)
            blk, par = c // 2, c % 2
            rr = c % 2
            if not parts & 1:
                return
            for wv in range(2):
                hps = list(range(wv * 4, wv * 4 + 4))
                slots = {}
                for hp in hps:
                    koff = (hp // 2) * TT + c * CH
                    klo = (hp % 2) * 64
                    kap = khat[klo:klo + 64, koff:koff + CH]
                    gp, b_gp = poolMV.alloc(64)
                    for half in range(2):
                        kb.emit("T", lambda e, kap=kap, gp=gp, half=half, klo=klo: e.matmul(
                            gp[half * 64:half * 64 + 64, :], kap, kap, start=True, stop=True,
                            tile_position=(klo, half * 64)),
                            reads=[b_khat[hp // 2]], writes=[b_gp])
                    p1, b_p1 = poolMS.alloc(128)
                    kb.emit("T", lambda e, hp=hp, p1=p1, c=c: e.matmul(
                        p1, selhp[:, hp * 128:(hp + 1) * 128],
                        hmP[:, c * 128:(c + 1) * 128],
                        start=True, stop=True),
                        reads=[cb["selhp"], b_hmP], writes=[b_p1])
                    kp, b_kp = poolMV.alloc(64)
                    kb.emit("T", lambda e, kp=kp, blk=blk, par=par, hp=hp: e.matmul(
                        kp, permd2[:, par * 128:(par + 1) * 128],
                        ktm[:, blk * 512 + (hp // 2) * 128 + (hp % 2) * 64:
                            blk * 512 + (hp // 2) * 128 + (hp % 2) * 64 + 64],
                        start=True, stop=True),
                        reads=[cb["permd2"], b_ktm], writes=[b_kp])
                    vp, b_vp = poolMV.alloc(64)
                    for half in range(2):
                        kb.emit("T", lambda e, vp=vp, blk=blk, par=par, hp=hp, half=half:
                                e.matmul(vp[half * 64:half * 64 + 64, :],
                                         permh[:, par * 64:(par + 1) * 64],
                                         vtm[:, blk * 1024 + hp * 128 + half * 64:
                                             blk * 1024 + hp * 128 + half * 64 + 64],
                                         start=True, stop=True, tile_position=(0, half * 64)),
                                reads=[cb["permh"], b_vtm], writes=[b_vp])
                    dp = b_dp = None
                    if is_own:
                        qoff = (hp // 2) * TOWN + co * CH
                        qap = qhat[klo:klo + 64, qoff:qoff + CH]
                        dp, b_dp = poolMV.alloc(64)
                        for half in range(2):
                            kb.emit("T", lambda e, kap=kap, qap=qap, dp=dp, half=half, klo=klo:
                                    e.matmul(dp[half * 64:half * 64 + 64, :], kap, qap,
                                             start=True, stop=True,
                                             tile_position=(klo, half * 64)),
                                    reads=[b_khat[hp // 2], b_qhat[hp // 2]], writes=[b_dp])
                    slots[hp] = (gp, b_gp, p1, b_p1, kp, b_kp, vp, b_vp, dp, b_dp)
                for hp in hps:
                    gp, b_gp, p1, b_p1, kp, b_kp, vp, b_vp, dp, b_dp = slots[hp]
                    ex, b_ex = wEx[hp], b_wEx[hp]
                    kb.emit("S", lambda e, ex=ex, p1=p1, c=c, hp=hp: e.activation(
                        ex[:, 0:64], p1[:, 0:64], AF.Exp, bias=XC(XI_NEGB, c, hp)),
                        reads=[b_p1, b_xcol], writes=[b_ex])
                    kb.emit("S", lambda e, ex=ex, p1=p1, c=c, hp=hp: e.activation(
                        ex[:, 64:128], p1[:, 64:128], AF.Exp, scale=-1.0,
                        bias=XC(XI_BLOGB, c, hp)),
                        reads=[b_p1, b_xcol], writes=[b_ex])
                    if is_own:
                        kb.emit("S", lambda e, ex=ex, p1=p1, c=c, hp=hp: e.activation(
                            ex[:, 128:192], p1[:, 64:128], AF.Exp, bias=XC(XI_NEGB, c, hp)),
                            reads=[b_p1, b_xcol], writes=[b_ex])
                        kb.emit("S", lambda e, p1=p1, hp=hp: e.activation(
                            wE1[hp][:], p1[:, 64:128], AF.Exp),
                            reads=[b_p1], writes=[b_wE1[hp]])
                for hp in hps:
                    gp, b_gp, p1, b_p1, kp, b_kp, vp, b_vp, dp, b_dp = slots[hp]
                    ex, b_ex = wEx[hp], b_wEx[hp]
                    kb.emit("V", lambda e, hp=hp, ex=ex: e.copy_predicated(
                        exmNT[hp][:], cn["mus"][:], ex[:, 0:64]),
                        reads=[cb["mus"], b_ex], writes=[b_exmNT[hp]])
                    kb.emit("V", lambda e, hp=hp, ex=ex: e.copy_predicated(
                        exmNP[hp][:], cn["mls"][:], ex[:, 64:128]),
                        reads=[cb["mls"], b_ex], writes=[b_exmNP[hp]])
                    kb.emit("V", lambda e, gp=gp, hp=hp, rr=rr: e.scalar_tensor_tensor(
                        wNT[hp][:, rr * 64:rr * 64 + 64], exmNT[hp][:], -1.0, gp,
                        ALU.mult, ALU.mult),
                        reads=[b_exmNT[hp], b_gp], writes=[b_wNT[hp]])
                    kb.emit("V", lambda e, gp=gp, hp=hp, rr=rr: e.scalar_tensor_tensor(
                        wNP[hp][:, rr * 64:rr * 64 + 64], exmNP[hp][:], -1.0, gp,
                        ALU.mult, ALU.mult),
                        reads=[b_exmNP[hp], b_gp], writes=[b_wNP[hp]])
                    if is_own:
                        kb.emit("V", lambda e, hp=hp, ex=ex: e.copy_predicated(
                            exmFTI[hp][:], cn["mui"][:], ex[:, 128:192]),
                            reads=[cb["mui"], b_ex], writes=[b_exmFTI[hp]])
                        kb.emit("V", lambda e, dp=dp, hp=hp: e.tensor_tensor(
                            wDT[hp][:], exmFTI[hp][:], dp, ALU.mult),
                            reads=[b_exmFTI[hp], b_dp], writes=[b_wDT[hp]])
                    kb.emit("V", lambda e, hp=hp, kp=kp, c=c, rr=rr: e.tensor_scalar(
                        wR[hp][:, rr * 128:rr * 128 + 64], kp, XC(XI_EBLOGB, c, hp), None,
                        ALU.mult),
                        reads=[b_kp, b_xcol], writes=[b_wR[hp]])
                    kb.emit("V", lambda e, hp=hp, vp=vp, c=c, rr=rr: e.tensor_scalar(
                        wR[hp][:, rr * 128 + 64:rr * 128 + 128], vp, XC(XI_BETA, c, hp), None,
                        ALU.mult),
                        reads=[b_vp, b_xcol], writes=[b_wR[hp]])
                    kb.emit("V", lambda e, hp=hp, kp=kp, c=c: e.tensor_scalar(
                        wKB[hp][:], kp, XC(XI_EBAR, c, hp), None, ALU.mult),
                        reads=[b_kp, b_xcol], writes=[b_wKB[hp]])
            cur = rr
            if not parts & 2:
                return
            for j in range(NLEV):
                nxt = 1 - cur
                slots = {}
                for hp in range(8):
                    rp, b_rp = poolMV.alloc(128)
                    for half in range(2):
                        lo = half * 64
                        kb.emit("T", lambda e, hp=hp, lo=lo, rp=rp, cur=cur: e.matmul(
                            rp[lo:lo + 64, :], wNT[hp][lo:lo + 64, cur * 64:cur * 64 + 64],
                            wR[hp][lo:lo + 64, cur * 128:cur * 128 + 128],
                            start=True, stop=True, tile_position=(lo, lo)),
                            reads=[b_wNT[hp], b_wR[hp]], writes=[b_rp])
                    sq1 = sq2 = b_sq1 = b_sq2 = None
                    if j < NLEV - 1:
                        sq1, b_sq1 = poolMS.alloc(64)
                        if j < NLEV - 2:
                            sq2, b_sq2 = poolMS.alloc(64)
                        for half in range(2):
                            lo = half * 64
                            kb.emit("T", lambda e, hp=hp, lo=lo, sq1=sq1, cur=cur: e.matmul(
                                sq1[lo:lo + 64, :],
                                wNP[hp][lo:lo + 64, cur * 64:cur * 64 + 64],
                                wNT[hp][lo:lo + 64, cur * 64:cur * 64 + 64],
                                start=True, stop=True, tile_position=(lo, lo)),
                                reads=[b_wNP[hp], b_wNT[hp]], writes=[b_sq1])
                            if sq2 is not None:
                                kb.emit("T", lambda e, hp=hp, lo=lo, sq2=sq2, cur=cur:
                                        e.matmul(
                                            sq2[lo:lo + 64, :],
                                            wNT[hp][lo:lo + 64, cur * 64:cur * 64 + 64],
                                            wNP[hp][lo:lo + 64, cur * 64:cur * 64 + 64],
                                            start=True, stop=True, tile_position=(lo, lo)),
                                        reads=[b_wNP[hp], b_wNT[hp]], writes=[b_sq2])
                    slots[hp] = (rp, b_rp, sq1, b_sq1, sq2, b_sq2)
                for hp in range(8):
                    rp, b_rp, sq1, b_sq1, sq2, b_sq2 = slots[hp]
                    kb.emit("V", lambda e, hp=hp, rp=rp, cur=cur, nxt=nxt:
                            e.tensor_tensor(
                                wR[hp][:, nxt * 128:nxt * 128 + 128], rp,
                                wR[hp][:, cur * 128:cur * 128 + 128], ALU.add),
                            reads=[b_rp, b_wR[hp]], writes=[b_wR[hp]])
                    if sq1 is not None:
                        kb.emit("S", lambda e, hp=hp, sq1=sq1, nxt=nxt: e.copy(
                            wNT[hp][:, nxt * 64:nxt * 64 + 64], sq1),
                            reads=[b_sq1], writes=[b_wNT[hp]])
                    if sq2 is not None:
                        kb.emit("S", lambda e, hp=hp, sq2=sq2, nxt=nxt: e.copy(
                            wNP[hp][:, nxt * 64:nxt * 64 + 64], sq2),
                            reads=[b_sq2], writes=[b_wNP[hp]])
                cur = nxt
            rfin = cur
            if not parts & 4:
                return
            for hp in range(8):
                Y1 = wR[hp][:, rfin * 128:rfin * 128 + 64]
                pt_, b_pt = poolMV.alloc(64)
                for half in range(2):
                    lo = half * 64
                    kb.emit("T", lambda e, hp=hp, lo=lo, pt_=pt_, Y1=Y1: e.matmul(
                        pt_[lo:lo + 64, :], Y1[lo:lo + 64, :], wKB[hp][lo:lo + 64, :],
                        start=True, stop=True, tile_position=(lo, lo)),
                        reads=[b_wR[hp], b_wKB[hp]], writes=[b_pt])
                kb.emit("V", lambda e, hp=hp, pt_=pt_, c=c: e.scalar_tensor_tensor(
                    wPT[hp][:], cn["is2"][:], EBCC(c, hp), pt_, ALU.mult, ALU.subtract),
                    reads=[cb["is2"], b_ebchp, b_pt], writes=[b_wPT[hp]])
                if is_own:
                    qe_, b_qe = poolMV.alloc(64)
                    for half in range(2):
                        lo = half * 64
                        kb.emit("T", lambda e, hp=hp, lo=lo, qe_=qe_, Y1=Y1: e.matmul(
                            qe_[lo:lo + 64, :], Y1[lo:lo + 64, :], wDT[hp][lo:lo + 64, :],
                            start=True, stop=True, tile_position=(lo, lo)),
                            reads=[b_wR[hp], b_wDT[hp]], writes=[b_qe])
                    klo = (hp % 2) * 64
                    qoff = (hp // 2) * TOWN + co * CH
                    qap = qhat[klo:klo + 64, qoff:qoff + CH]
                    qd, b_qd = poolMV.alloc(64)
                    for half in range(2):
                        kb.emit("T", lambda e, qap=qap, qd=qd, half=half, klo=klo:
                                e.matmul(qd[half * 64:half * 64 + 64, :],
                                         cn["identb"][klo:klo + 64, klo:klo + 64], qap,
                                         start=True, stop=True,
                                         tile_position=(klo, half * 64)),
                                reads=[cb["identb"], b_qhat[hp // 2]], writes=[b_qd])
                    kb.emit("V", lambda e, hp=hp, qd=qd: e.tensor_tensor(
                        wGL[hp][:], qd, wE1[hp][:], ALU.mult),
                        reads=[b_qd, b_wE1[hp]], writes=[b_wGL[hp]])
                    kb.emit("V", lambda e, hp=hp, qe_=qe_: e.tensor_tensor(
                        wQE[hp][:], wGL[hp][:], qe_, ALU.subtract),
                        reads=[b_wGL[hp], b_qe], writes=[b_wQE[hp]])
            if not parts & 8:
                return
            for hp in range(8):
                sring = psA if hp % 2 == 0 else psB
                b_sring = b_psA if hp % 2 == 0 else b_psB
                oring = poA if hp % 2 == 0 else poB
                b_oring = b_poA if hp % 2 == 0 else b_poB
                so = (hp // 2) * 64
                Y2 = wR[hp][:, rfin * 128 + 64:rfin * 128 + 128]
                sb16 = Sb16[hp]
                for half in range(2):
                    lo = half * 64
                    kb.emit("T", lambda e, hp=hp, lo=lo, so=so, Y2=Y2, sring=sring:
                            e.matmul(sring[lo:lo + 64, so:so + 64],
                                     wKB[hp][lo:lo + 64, :], Y2[lo:lo + 64, :],
                                     start=True, stop=False, tile_position=(lo, lo)),
                            reads=[b_wKB[hp], b_wR[hp]], writes=[b_sring])
                    kb.emit("T", lambda e, hp=hp, lo=lo, so=so, sb16=sb16, sring=sring:
                            e.matmul(sring[lo:lo + 64, so:so + 64],
                                     wPT[hp][lo:lo + 64, :], sb16[lo:lo + 64, :],
                                     start=False, stop=True, tile_position=(lo, lo)),
                            reads=[b_wPT[hp], b_Sb16[hp]], writes=[b_sring])
                if is_own:
                    for half in range(2):
                        lo = half * 64
                        kb.emit("T", lambda e, hp=hp, lo=lo, so=so, Y2=Y2, oring=oring:
                                e.matmul(oring[lo:lo + 64, so:so + 64],
                                         Y2[lo:lo + 64, :], wDT[hp][lo:lo + 64, :],
                                         start=True, stop=False, tile_position=(lo, lo)),
                                reads=[b_wR[hp], b_wDT[hp]], writes=[b_oring])
                        kb.emit("T", lambda e, hp=hp, lo=lo, so=so, sb16=sb16, oring=oring:
                                e.matmul(oring[lo:lo + 64, so:so + 64],
                                         sb16[lo:lo + 64, :], wQE[hp][lo:lo + 64, :],
                                         start=False, stop=True, tile_position=(lo, lo)),
                                reads=[b_Sb16[hp], b_wQE[hp]], writes=[b_oring])
            for hp in range(8):
                sring = psA if hp % 2 == 0 else psB
                b_sring = b_psA if hp % 2 == 0 else b_psB
                oring = poA if hp % 2 == 0 else poB
                b_oring = b_poA if hp % 2 == 0 else b_poB
                so = (hp // 2) * 64
                sold = Sst[hp][:, (c % 2) * 64:(c % 2) * 64 + 64]
                snew = Sst[hp][:, (1 - c % 2) * 64:(1 - c % 2) * 64 + 64]
                sb16 = Sb16[hp]
                kb.emit("V", lambda e, hp=hp, so=so, sold=sold, snew=snew, c=c,
                        sring=sring: e.scalar_tensor_tensor(
                            snew, sold, EBCC(c, hp), sring[:, so:so + 64],
                            ALU.mult, ALU.add),
                        reads=[b_Sst[hp], b_ebchp, b_sring], writes=[b_Sst[hp]])
                kb.emit("V", lambda e, hp=hp, snew=snew, sb16=sb16: e.tensor_copy(
                    sb16, snew), reads=[b_Sst[hp]], writes=[b_Sb16[hp]])
                if is_own:
                    kb.emit("S", lambda e, hp=hp, so=so, co=co, oring=oring: e.copy(
                        ohat[:, hp * TOWN + co * CH:hp * TOWN + (co + 1) * CH],
                        oring[:, so:so + 64]),
                        reads=[b_oring], writes=[b_ohat[hp]])


        kb.on = phase >= 5
        w0cast = nc.dram_tensor("w0cast", [64, 128, 16 * 128], BF16,
                                kind="Internal").ap()
        w1cast = nc.dram_tensor("w1cast", [64, 4, 128, 512], BF16,
                                kind="Internal").ap()
        wocast = nc.dram_tensor("wocast", [16, 128, 8 * 128], BF16,
                                kind="Internal").ap()
        wpcast = nc.dram_tensor("wpcast", [8, 128, 16 * 128], BF16,
                                kind="Internal").ap()
        b_w0c, b_w1c = Buf("w0c"), Buf("w1c")
        b_woc, b_wpc = Buf("woc"), Buf("wpc")
        for oc in range(16):
            kb.emit("G", lambda e, oc=oc: e.dma_start(
                wocast[oc].rearrange("p (f c) -> p f c", c=128),
                dr["woutT"][:, oc * 128:(oc + 1) * 128]
                .rearrange("(f p) c -> p f c", p=128)),
                writes=[b_woc], dma=True)
        for j in range(8):
            oc = PC_Z[0] + j
            kb.emit("G", lambda e, j=j, oc=oc: e.dma_start(
                wpcast[j], dr["wproj"][oc * 128:(oc + 1) * 128, :]),
                writes=[b_wpc], dma=True)
        for nb in range(64):
            kb.emit("G", lambda e, nb=nb: e.dma_start(
                w0cast[nb].rearrange("p (f c) -> p f c", c=128),
                dr["w0T"][:, nb * 128:(nb + 1) * 128]
                .rearrange("(f p) c -> p f c", p=128)),
                writes=[b_w0c], dma=True)
        for np_ in range(64):
            kb.emit("G", lambda e, np_=np_: e.dma_start(
                w1cast[np_], dr["w1T"][np_ * 128:(np_ + 1) * 128, :]
                .rearrange("p (o c) -> o p c", c=512)),
                writes=[b_w1c], dma=True)
        for c in range(min(DBG["mach_n"], NCH_ALL)):
            mach_chunk(c)

        kb.on = phase >= 6
        for i in range(2):
            inherit(b_wbuf[i], [b_ktm])
            inherit(b_xslab[i], [b_vtm])
        for bz in b_zsilu:
            inherit(bz, mach_bufs)
        proj_pass(PC_Z, dr["xin"], THALO, TOWN)

        ssO = af(O_SSO, TOWN, 0, 16)
        b_ssO = Buf("ssO")
        inherit(b_ssO, [b_xcol])
        for bb in b_sqsc:
            inherit(bb, [b_ktm])
        for sl in range(TOWN // SLAB):
            pt, b_pt = pool.alloc(SLAB)
            pt16 = pt[0:16, :]
            for hp in range(8):
                kb.emit("S", lambda e, hp=hp, sl=sl: e.activation(
                    sqsc[hp % 2][:],
                    ohat[:, hp * TOWN + sl * SLAB:hp * TOWN + (sl + 1) * SLAB], AF.Square),
                    reads=[b_ohat[hp]], writes=[b_sqsc[hp % 2]])
                kb.emit("T", lambda e, hp=hp, pt16=pt16: e.matmul(
                    pt16, bo16[:, hp * 16:(hp + 1) * 16], sqsc[hp % 2][:],
                    start=(hp == 0), stop=(hp == 7)),
                    reads=[cb["bo16"], b_sqsc[hp % 2]], writes=[b_pt])
            kb.emit("V", lambda e, sl=sl, pt16=pt16: e.tensor_copy(
                ssO[:, sl * SLAB:(sl + 1) * SLAB], pt16), reads=[b_pt], writes=[b_ssO])
        kb.emit("V", lambda e: e.tensor_scalar(ssO[:], ssO[:], 1.0 / 64.0, EPS_L2,
                                               ALU.mult, ALU.add),
                reads=[b_ssO], writes=[b_ssO])
        kb.emit("V", lambda e: e.reciprocal(ssO[:], ssO[:]), reads=[b_ssO], writes=[b_ssO])
        kb.emit("S", lambda e: e.activation(ssO[:], ssO[:], AF.Sqrt),
                reads=[b_ssO], writes=[b_ssO])
        nw1 = af(O_MISC + 2, 1)
        b_nw1 = Buf("nw1")
        kb.emit("V", lambda e: e.tensor_scalar(nw1[:], cn["nw2"][:], 1.0, None, ALU.add),
                reads=[cb["nw2"]], writes=[b_nw1])
        omov = abf(O_OMOV, 8 * TOWN)
        b_omov = [Buf(f"omov{h}") for h in range(8)]
        for bb in b_omov:
            inherit(bb, b_xslab + [b_vtm])
        for hp in range(8):
            for sl in range(TOWN // SLAB):
                pt, b_pt = pool.alloc(SLAB)
                kb.emit("T", lambda e, hp=hp, sl=sl, pt=pt: e.matmul(
                    pt, selhp[:, hp * 128:(hp + 1) * 128],
                    ssO[:, sl * SLAB:(sl + 1) * SLAB],
                    start=True, stop=True),
                    reads=[cb["selhp"], b_ssO], writes=[b_pt])
                off = hp * TOWN + sl * SLAB
                kb.emit("V", lambda e, hp=hp, off=off, pt=pt: e.scalar_tensor_tensor(
                    sqsc[0][:], ohat[:, off:off + SLAB], nw1[:], pt, ALU.mult, ALU.mult),
                    reads=[b_ohat[hp], b_nw1, b_pt], writes=[b_sqsc[0]])
                kb.emit("V", lambda e, hp=hp, off=off: e.tensor_tensor(
                    omov[:, off:off + SLAB], sqsc[0][:], zsilu[:, off:off + SLAB],
                    ALU.mult),
                    reads=[b_sqsc[0], b_zsilu[hp]], writes=[b_omov[hp]])

        xin = af(O_XIN, 16 * TOWN)
        b_xin = [Buf(f"xin{j}") for j in range(16)]
        for bb in b_xin:
            inherit(bb, [b_hmP] + b_wEx + b_Sst + [b_ssk, b_hm, b_tmp16])
        for j in range(16):
            kb.emit("Q", lambda e, j=j: e.dma_start(
                xin[:, j * TOWN:(j + 1) * TOWN],
                dr["xin"].rearrange("(f p) t -> p f t", p=128)[:, j, :]),
                writes=[b_xin[j]], dma=True)

        wobuf = [abf(O_WOBUF + i * 1024, 1024) for i in range(2)]
        b_wobuf = [Buf(f"wobuf{i}") for i in range(2)]
        for bb in b_wobuf:
            for bk in b_khat:
                inherit(bb, bk)

        def wopanel(oc, slot):
            src = wocast[oc]
            kb.emit("Q", lambda e, slot=slot, src=src: e.dma_start(
                wobuf[slot][:], src),
                reads=[b_woc], writes=[b_wobuf[slot]], dma=True)

        wopanel(0, 0)
        wopanel(1, 1)
        for oc in range(16):
            slot = oc % 2
            for sl in range(TOWN // SLAB):
                pt, b_pt = pool.alloc(SLAB)
                for vc in range(8):
                    kb.emit("T", lambda e, oc=oc, sl=sl, vc=vc, pt=pt, slot=slot: e.matmul(
                        pt, wobuf[slot][:, vc * 128:(vc + 1) * 128],
                        omov[:, vc * TOWN + sl * SLAB:vc * TOWN + (sl + 1) * SLAB],
                        start=(vc == 0), stop=(vc == 7)),
                        reads=[b_wobuf[slot], b_omov[vc]], writes=[b_pt])
                kb.emit("V", lambda e, oc=oc, sl=sl, pt=pt: e.tensor_tensor(
                    xin[:, oc * TOWN + sl * SLAB:oc * TOWN + (sl + 1) * SLAB],
                    xin[:, oc * TOWN + sl * SLAB:oc * TOWN + (sl + 1) * SLAB], pt,
                    ALU.add),
                    reads=[b_xin[oc], b_pt], writes=[b_xin[oc]])
            if oc + 2 < 16:
                wopanel(oc + 2, slot)

        y1b = abf(O_Y1B, 16 * TOWN)
        b_y1b = [Buf(f"y1b{j}") for j in range(16)]
        for bb in b_y1b:
            inherit(bb, [b_vtm] + b_omov + b_xslab)
        LNR = {"m": O_LNM, "ms": O_LNR, "r": O_LNR, "t1": O_LNT, "mr": O_LNT}

        def lnr(nm, sl=None):
            c0 = LNR[nm]
            if sl is None:
                return AFA[0:1, c0:c0 + TOWN]
            return AFA[0:1, c0 + sl * SLAB:c0 + (sl + 1) * SLAB]
        b_lnrow = Buf("lnrow")
        inherit(b_lnrow, [b_ssO, b_ebchp])

        def layernorm(make_bf16):
            for j in range(16):
                kb.emit("V", lambda e, j=j: e.tensor_copy(
                    y1b[:, j * TOWN:(j + 1) * TOWN], xin[:, j * TOWN:(j + 1) * TOWN]),
                    reads=[b_xin[j]], writes=[b_y1b[j]])
            for sl in range(TOWN // SLAB):
                pt, b_pt = pool.alloc(SLAB)
                pt1 = pt[0:1, :]
                for j in range(16):
                    kb.emit("T", lambda e, j=j, sl=sl, pt1=pt1: e.matmul(
                        pt1, cn["ones"][:, 0:1],
                        y1b[:, j * TOWN + sl * SLAB:j * TOWN + (sl + 1) * SLAB],
                        start=(j == 0), stop=(j == 15)),
                        reads=[cb["ones"], b_y1b[j]], writes=[b_pt])
                kb.emit("V", lambda e, sl=sl, pt1=pt1: e.tensor_scalar(
                    lnr("m", sl), pt1, 1.0 / H, None, ALU.mult),
                    reads=[b_pt], writes=[b_lnrow])
                pt2, b_pt2 = pool.alloc(SLAB)
                pt2a = pt2[0:1, :]
                for j in range(16):
                    kb.emit("S", lambda e, j=j, sl=sl: e.activation(
                        sqsc[j % 2][:],
                        y1b[:, j * TOWN + sl * SLAB:j * TOWN + (sl + 1) * SLAB],
                        AF.Square),
                        reads=[b_y1b[j]], writes=[b_sqsc[j % 2]])
                    kb.emit("T", lambda e, j=j, pt2a=pt2a: e.matmul(
                        pt2a, cn["ones"][:, 0:1], sqsc[j % 2][:],
                        start=(j == 0), stop=(j == 15)),
                        reads=[cb["ones"], b_sqsc[j % 2]], writes=[b_pt2])
                kb.emit("V", lambda e, sl=sl, pt2a=pt2a: e.tensor_scalar(
                    lnr("ms", sl), pt2a, 1.0 / H, None, ALU.mult),
                    reads=[b_pt2], writes=[b_lnrow])
            kb.emit("V", lambda e: e.tensor_tensor(
                lnr("t1"), lnr("m"), lnr("m"), ALU.mult),
                reads=[b_lnrow], writes=[b_lnrow])
            kb.emit("V", lambda e: e.tensor_tensor(
                lnr("r"), lnr("ms"), lnr("t1"), ALU.subtract),
                reads=[b_lnrow], writes=[b_lnrow])
            kb.emit("V", lambda e: e.tensor_scalar(
                lnr("r"), lnr("r"), EPS_LN, None, ALU.add),
                reads=[b_lnrow], writes=[b_lnrow])
            kb.emit("V", lambda e: e.reciprocal(lnr("r"), lnr("r")),
                reads=[b_lnrow], writes=[b_lnrow])
            kb.emit("S", lambda e: e.activation(lnr("r"), lnr("r"), AF.Sqrt),
                reads=[b_lnrow], writes=[b_lnrow])
            kb.emit("V", lambda e: e.tensor_tensor(
                lnr("mr"), lnr("m"), lnr("r"), ALU.mult),
                reads=[b_lnrow], writes=[b_lnrow])
            for sl in range(TOWN // SLAB):
                ptr_, b_ptr = pool.alloc(SLAB)
                ptm_, b_ptm = pool.alloc(SLAB)
                kb.emit("T", lambda e, sl=sl, ptr_=ptr_: e.matmul(
                    ptr_, cn["onesf"][0:1, :], lnr("r", sl),
                    start=True, stop=True),
                    reads=[cb["onesf"], b_lnrow], writes=[b_ptr])
                kb.emit("T", lambda e, sl=sl, ptm_=ptm_: e.matmul(
                    ptm_, cn["onesf"][0:1, :], lnr("mr", sl),
                    start=True, stop=True),
                    reads=[cb["onesf"], b_lnrow], writes=[b_ptm])
                for j in range(16):
                    xs = xin[:, j * TOWN + sl * SLAB:j * TOWN + (sl + 1) * SLAB]
                    kb.emit("V", lambda e, xs=xs, ptr_=ptr_: e.tensor_tensor(
                        xs, xs, ptr_, ALU.mult),
                        reads=[b_xin[j], b_ptr], writes=[b_xin[j]])
                    kb.emit("V", lambda e, xs=xs, ptm_=ptm_: e.tensor_tensor(
                        xs, xs, ptm_, ALU.subtract),
                        reads=[b_xin[j], b_ptm], writes=[b_xin[j]])
                    if make_bf16:
                        kb.emit("V", lambda e, j=j, sl=sl, xs=xs: e.tensor_copy(
                            y1b[:, j * TOWN + sl * SLAB:j * TOWN + (sl + 1) * SLAB], xs),
                            reads=[b_xin[j]], writes=[b_y1b[j]])

        kb.on = True
        if debug:
            kb.emit("G", lambda e: e.dma_start(dbg["d_khat"], khat[:]),
                    reads=b_khat, dma=True)
            kb.emit("G", lambda e: e.dma_start(dbg["d_qx"], qhat[:, :]),
                    reads=b_qhat, dma=True)
            kb.emit("G", lambda e: e.dma_start(dbg["d_ohat"], ohat[:]),
                    reads=b_ohat, dma=True)
            kb.emit("G", lambda e: e.dma_start(dbg["d_zs"], zsilu[:]),
                    reads=b_zsilu, dma=True)
            kb.emit("G", lambda e: e.dma_start(dbg["d_ktm"], ktm[:]),
                    reads=[b_ktm], dma=True)
            kb.emit("G", lambda e: e.dma_start(dbg["d_xcol"], xcol[:]),
                    reads=[b_xcol], dma=True)
            kb.emit("G", lambda e: e.dma_start(dbg["d_bg"], AFA[0:16, 0:5 * TT]),
                    reads=[b_hm], dma=True)
            kb.emit("G", lambda e: e.dma_start(dbg["d_attn"], xin[:]),
                    reads=b_xin, dma=True)

        kb.on = phase >= 7
        layernorm(make_bf16=True)
        kb.on = True

        if debug:
            kb.emit("G", lambda e: e.dma_start(dbg["d_y1"], xin[:]),
                    reads=b_xin, dma=True)

        kb.on = phase >= 8
        kb.barrier()
        Dff = abf(O_DFF, 64 * SLAB)
        b_Dff = [Buf(f"D{nc_}") for nc_ in range(64)]
        w0blk = [abf(O_W0B + i * 2048, 16 * 128) for i in range(2)]
        b_w0blk = [Buf(f"w0blk{i}") for i in range(2)]
        w1buf = [abf(O_W1B + i * SLAB, SLAB) for i in range(4)]
        b_w1buf = [Buf(f"w1buf{i}") for i in range(4)]
        ffnpool = PPool([(psB, 0, 512, b_psB), (poA, 0, 512, b_poA),
                         (poB, 0, 512, b_poB)])

        W0C = 128

        def w0panel(nb, slot):
            src = w0cast[nb]
            kb.emit("Q", lambda e, slot=slot, src=src: e.dma_start(
                w0blk[slot][:], src),
                reads=[b_w0c], writes=[b_w0blk[slot]], dma=True)

        def w1panel(ocg, nc_, slot):
            src = w1cast[nc_, ocg]
            kb.emit("Q", lambda e, slot=slot, src=src: e.dma_start(w1buf[slot][:], src),
                    reads=[b_w1c], writes=[b_w1buf[slot]], dma=True)

        for halfT in range(2):
            t0 = halfT * SLAB
            w0panel(0, 0)
            w0panel(1, 1)
            for nb in range(64):
                slot = nb % 2
                nc_ = nb
                pt, b_pt = ffnpool.alloc(SLAB)
                for f in range(16):
                    kb.emit("T", lambda e, f=f, pt=pt, slot=slot: e.matmul(
                        pt, w0blk[slot][:, f * W0C:(f + 1) * W0C],
                        y1b[:, f * TOWN + t0:f * TOWN + t0 + SLAB],
                        start=(f == 0), stop=(f == 15)),
                        reads=[b_w0blk[slot]] + [b_y1b[f]], writes=[b_pt])
                kb.emit("S", lambda e, nc_=nc_, pt=pt: e.activation(
                    Dff[:, nc_ * SLAB:(nc_ + 1) * SLAB], pt, AF.Gelu,
                    bias=cn["bias0"][:, nc_:nc_ + 1]),
                    reads=[b_pt, cb["bias0"]], writes=[b_Dff[nc_]])
                if nb + 2 < 64:
                    w0panel(nb + 2, slot)
            for ocg in range(4):
                for k in range(4):
                    w1panel(ocg, k, k)
                galloc = [(pp[:, 0:512], b_pp0), (pp[:, 512:1024], b_pp1),
                          (pm[:, 0:512], b_pm), (psA[:, 0:512], b_psA)]
                for nc_ in range(64):
                    slot = nc_ % 4
                    for oc4 in range(4):
                        gp, b_gp = galloc[oc4]
                        kb.emit("T", lambda e, nc_=nc_, oc4=oc4, gp=gp, slot=slot:
                                e.matmul(gp, w1buf[slot][:, oc4 * 128:(oc4 + 1) * 128],
                                         Dff[:, nc_ * SLAB:(nc_ + 1) * SLAB],
                                         start=(nc_ == 0), stop=(nc_ == 63)),
                                reads=[b_w1buf[slot], b_Dff[nc_]], writes=[b_gp])
                    if nc_ + 4 < 64:
                        w1panel(ocg, nc_ + 4, slot)
                for oc4 in range(4):
                    oc = ocg * 4 + oc4
                    gp, b_gp = galloc[oc4]
                    xs = xin[:, oc * TOWN + t0:oc * TOWN + t0 + SLAB]
                    kb.emit("V", lambda e, oc=oc, gp=gp, xs=xs: e.scalar_tensor_tensor(
                        xs, gp, cn["bias1"][:, oc:oc + 1], xs, ALU.add, ALU.add),
                        reads=[b_gp, cb["bias1"], b_xin[oc]], writes=[b_xin[oc]])

        for bb in b_sqsc:
            inherit(bb, b_w0blk)
        layernorm(make_bf16=False)
        for j in range(16):
            kb.emit("Q", lambda e, j=j: e.dma_start(
                yout_d.rearrange("(f p) t -> p f t", p=128)[:, j, :],
                xin[:, j * TOWN:(j + 1) * TOWN]),
                reads=[b_xin[j]], dma=True)
        kb.finish()

    return nc


_cache = {}


def kernel(**inputs):
    in_maps = host_prep(inputs)
    key = "prod"
    if key not in _cache:
        _cache[key] = build_nc(debug=False)
    nc = _cache[key]
    res = run_bass_kernel_spmd(nc, in_maps, core_ids=list(range(8)))
    x = inputs["L0_attn_hidden_states"]
    out = np.empty((B, S, H), dtype=np.float32)
    for c in range(8):
        b, half = c // 2, c % 2
        out[b, half * TOWN:(half + 1) * TOWN, :] = np.asarray(res.results[c]["yout"]).T
    return out


if __name__ == "__main__":
    print("kernel module OK")

